# version 02 (final)
# baseline (speedup 1.0000x reference)
import contextlib
import numpy as np
import ml_dtypes
import concourse.bass as bass
import concourse.mybir as mybir
from concourse.bass_utils import run_bass_kernel_spmd

F32 = mybir.dt.float32
BF16 = mybir.dt.bfloat16
AF = mybir.ActivationFunctionType
ALU = mybir.AluOpType
BF = ml_dtypes.bfloat16

NCORES = 8
T = 1024
NT = 8
HID = 2048
I = 4096
CONV = 4352
PROJ = 8512
INTER = 8192
EPS = 1e-5

MC_GATE = 32
MC_HBC = 34
MC_ALL = 67
HBW = 1030


class Sem:
    def __init__(self, stack, nc, name):
        self.h = stack.enter_context(nc.semaphore(name))
        self.v = 0

    def inc(self, inst, n=1):
        inst.then_inc(self.h, n)
        self.v += n
        return self.v


def build_program(taps=(), stop_after=99):
    taps = set(taps)
    nc = bass.Bass("TRN2", target_bir_lowering=False)

    x_hbm = nc.dram_tensor("x_tok", [T, HID], F32, kind="ExternalInput").ap()
    xprev_hbm = nc.dram_tensor("xprev", [3, HID], F32, kind="ExternalInput").ap()
    win_hbm = nc.dram_tensor("w_in_s", [MC_ALL, 128, 2048], F32, kind="ExternalInput").ap()
    wout_hbm = nc.dram_tensor("w_out_s", [16, 128, 4096], F32, kind="ExternalInput").ap()
    gu_hbm = nc.dram_tensor("w_gu_s", [128, 128, 2048], F32, kind="ExternalInput").ap()
    dn_hbm = nc.dram_tensor("w_dn_s", [4, 64, 128, 512], F32, kind="ExternalInput").ap()
    convw_hbm = nc.dram_tensor("conv_w", [128, 136], F32, kind="ExternalInput").ap()
    convb_hbm = nc.dram_tensor("conv_b", [128, 34], F32, kind="ExternalInput").ap()
    dtb_hbm = nc.dram_tensor("dt_bias", [64, 1], F32, kind="ExternalInput").ap()
    alog_hbm = nc.dram_tensor("a_log", [64, 1], F32, kind="ExternalInput").ap()
    dN_hbm = nc.dram_tensor("d_n", [128, 64], F32, kind="ExternalInput").ap()
    normw_hbm = nc.dram_tensor("normw_g", [128, 32], F32, kind="ExternalInput").ap()
    emask_hbm = nc.dram_tensor("emask", [128, 16], F32, kind="ExternalInput").ap()

    out_hbm = nc.dram_tensor("out", [T, HID], F32, kind="ExternalOutput").ap()

    sg_dram = nc.dram_tensor("sg_scr", [MC_GATE, 128, T], BF16, kind="Internal").ap()
    x2_dram = nc.dram_tensor("x2_scr", [T, HID], F32, kind="Internal").ap()

    e128 = np.eye(128)
    id128b_h = nc.inline_tensor(e128.astype(BF), "c_id128b")
    id64b_h = nc.inline_tensor(np.eye(64).astype(BF), "c_id64b")
    id128f_h = nc.inline_tensor(e128.astype(np.float32), "c_id128f")
    id64f_h = nc.inline_tensor(np.eye(64, dtype=np.float32), "c_id64f")
    trigt_h = nc.inline_tensor(np.tril(np.ones((128, 128)), -1).astype(BF), "c_trigt")
    trile_h = nc.inline_tensor(np.triu(np.ones((128, 128)), 0).astype(BF), "c_trile")
    trige_h = nc.inline_tensor(np.triu(np.ones((128, 128)), 0).astype(np.float32), "c_trige")

    tap_shapes = {}
    phase_stacks = []

    def _mkstack():
        st = contextlib.ExitStack()
        phase_stacks.append(st)
        return st

    def _close_phase_stacks():
        for st in reversed(phase_stacks):
            st.close()

    es = contextlib.ExitStack()
    with es:
        E = es.enter_context

        def sb(name, shape, dtype, side="left"):
            return nc.sbuf_tensor(name, shape, dtype, side=side)

        id128b = E(sb("id128b", [128, 128], BF16))
        id64b = E(sb("id64b", [64, 64], BF16))
        id128f = E(sb("id128f", [128, 128], F32))
        id64f = E(sb("id64f", [64, 64], F32))
        trigt = E(sb("trigt", [128, 128], BF16))
        trile = E(sb("trile", [128, 128], BF16))
        trige = E(sb("trige", [128, 128], F32))
        convw = E(sb("sb_convw", [128, 136], F32))
        convb = E(sb("sb_convb", [128, 34], F32))
        dtb = E(sb("dtb", [64, 1], F32))
        alog = E(sb("alog", [64, 1], F32))
        negA = E(sb("negA", [64, 1], F32))
        d_n = E(sb("sb_d_n", [128, 64], F32))
        normw = E(sb("sb_normw", [128, 32], F32))
        emask = E(sb("sb_emask", [128, 16], F32))
        ones64 = E(sb("ones64", [64, 1], F32))
        dA_b = E(sb("dA_b", [64, T], BF16))
        lndt_b = E(sb("lndt_b", [64, T], BF16))
        acumT = E(sb("acumT", [64, T], F32))
        alsum = E(sb("alsum", [64, 1], F32))
        rinv8 = E(sb("rinv8", [128, 8], F32))
        dcrN = E(sb("dcrN", [128, 64], F32))
        sums16 = E(sb("sums16", [128, 16], F32))
        ssq16 = E(sb("ssq16", [128, 16], F32))
        mucol = E(sb("mucol", [128, 16], F32))
        rstd = E(sb("rstd", [128, 16], F32))

        pF = E(nc.psum_tensor("pF", [128, 2560], F32))
        pX = E(nc.psum_tensor("pX", [128, 512], F32))
        pTB = E(nc.psum_tensor("pTB", [128, 2048], BF16))

        LD = Sem(es, nc, "LD")
        LDX = [Sem(es, nc, "LDX0"), Sem(es, nc, "LDX1")]
        WLD = Sem(es, nc, "WLD")
        WLDP = [Sem(es, nc, "WLDP0"), Sem(es, nc, "WLDP1")]
        DNP = [Sem(es, nc, "DNP0"), Sem(es, nc, "DNP1")]
        CVT = Sem(es, nc, "CVT")
        MM = Sem(es, nc, "MM")
        EVV = Sem(es, nc, "EVV")
        EVS = Sem(es, nc, "EVS")
        EVP = Sem(es, nc, "EVP")
        SPL = Sem(es, nc, "SPL")
        CC = Sem(es, nc, "CC")
        GDM = Sem(es, nc, "GDM")
        STO = Sem(es, nc, "STO")

        block = E(nc.Block())

        def tap(name, ap_src, wait_sems):
            if name not in taps:
                return
            th = nc.dram_tensor("tap_" + name, list(ap_src.shape), ap_src.dtype,
                                kind="ExternalOutput")
            tap_shapes[name] = tuple(ap_src.shape)

            @block.sync
            def _(sync):
                for s, v in wait_sems:
                    sync.wait_ge(s.h, v)
                STO.inc(sync.dma_start(out=th.ap(), in_=ap_src), 16)

        esDT = _mkstack()
        EDT = esDT.enter_context
        dt_raw = EDT(sb("dt_raw", [64, T], F32))
        dt_f = EDT(sb("dt_f", [64, T], F32))
        dA_f = EDT(sb("dA_f", [64, T], F32))
        tsc64 = EDT(sb("tsc64", [64, T], F32))

        esHBC = _mkstack()
        hbc = esHBC.enter_context(sb("hbc", [128, MC_HBC * HBW], BF16))

        esAB = _mkstack()
        EA = esAB.enter_context
        x_bf = EA(sb("x_bf", [128, 16 * T], BF16, side="right"))
        xstg = [EA(sb(f"xstg{i}", [128, HID], F32, side="right")) for i in range(2)]
        xstgb = EA(sb("xstgb", [128, HID], BF16, side="right"))
        xprevstg = EA(sb("xprevstg", [3, HID], F32, side="right"))
        xprevstgb = EA(sb("xprevstgb", [3, HID], BF16, side="right"))
        xprevf = EA(sb("xprevf", [128, 64], BF16, side="right"))
        wraw = [EA(sb(f"wraw{i}", [128, 2048], F32, side="right")) for i in range(2)]
        wbf = [EA(sb(f"wbf{i}", [128, 2048], BF16, side="right")) for i in range(2)]
        sgf = EA(sb("sgf", [128, 512], BF16, side="right"))
        sgf2 = EA(sb("sgf2", [128, 512], BF16, side="right"))

        @block.sync
        def _(sync):
            for dst, src in (
                (id128b, id128b_h.ap()), (id64b, id64b_h.ap()), (id128f, id128f_h.ap()),
                (id64f, id64f_h.ap()),
                (trigt, trigt_h.ap()), (trile, trile_h.ap()), (trige, trige_h.ap()),
                (convw, convw_hbm), (convb, convb_hbm), (dtb, dtb_hbm),
                (alog, alog_hbm), (d_n, dN_hbm), (normw, normw_hbm),
                (emask, emask_hbm), (xprevstg, xprev_hbm),
            ):
                LD.inc(sync.dma_start(out=dst[:], in_=src), 16)
        ld_const = LD.v

        @block.vector
        def _(vector):
            vector.wait_ge(LD.h, ld_const)
            vector.memset(ones64[:], 1.0)
            EVV.inc(vector.tensor_copy(out=xprevstgb[:], in_=xprevstg[:]))
        ev_xprevc = EVV.v

        @block.tensor
        def _(tensor):
            tensor.wait_ge(EVV.h, ev_xprevc)
            last = None
            for kk in range(16):
                last = tensor.transpose(out=pTB[:, kk * 4:kk * 4 + 3],
                                 in_=xprevstgb[0:3, kk * 128:(kk + 1) * 128],
                                 identity=id128b[0:3, 0:3])
            MM.inc(last)
        mm_xprev = MM.v

        @block.vector
        def _(vector):
            vector.wait_ge(MM.h, mm_xprev)
            EVV.inc(vector.tensor_copy(
                out=xprevf[:].rearrange("q (k w) -> q k w", k=16)[:, :, 0:3],
                in_=pTB[:, 0:64].rearrange("q (k w) -> q k w", k=16)[:, :, 0:3]))
        ev_xprevf = EVV.v

        ld_x = [0] * NT
        ev_xcast = [0] * NT
        mm_xtr = [0] * NT
        ev_xevac = [0] * NT
        for i in range(NT):
            @block.sync
            def _(sync, i=i):
                if i >= 2:
                    sync.wait_ge(EVV.h, ev_xcast[i - 2])
                ld_x[i] = LDX[i % 2].inc(sync.dma_start(
                    out=xstg[i % 2][:], in_=x_hbm[i * 128:(i + 1) * 128, :]), 16)

            @block.vector
            def _(vector, i=i):
                vector.wait_ge(LDX[i % 2].h, ld_x[i])
                if i >= 1:
                    vector.wait_ge(MM.h, mm_xtr[i - 1])
                ev_xcast[i] = EVV.inc(vector.tensor_copy(out=xstgb[:], in_=xstg[i % 2][:]))

            @block.tensor
            def _(tensor, i=i):
                tensor.wait_ge(EVV.h, ev_xcast[i])
                tensor.wait_ge(EVV.h, ev_xevac[i - 1] if i >= 1 else ev_xprevf)
                last = None
                for k in range(16):
                    last = tensor.transpose(out=pTB[:, k * 128:(k + 1) * 128],
                                     in_=xstgb[:, k * 128:(k + 1) * 128],
                                     identity=id128b[:])
                mm_xtr[i] = MM.inc(last)

            @block.vector
            def _(vector, i=i):
                vector.wait_ge(MM.h, mm_xtr[i])
                dst = x_bf[:].rearrange("q (k t) -> q k t", k=16)[:, :, i * 128:(i + 1) * 128]
                ev_xevac[i] = EVV.inc(vector.tensor_copy(
                    out=dst, in_=pTB[:].rearrange("q (k t) -> q k t", k=16)))


        if stop_after < 1:
            _close_phase_stacks()
            return nc, tap_shapes

        wld = [0] * MC_ALL
        cvt_done = [0] * MC_ALL
        mm_strip = [0] * MC_ALL
        mme = [[0, 0] for _ in range(MC_ALL)]
        evd = {}
        spl_prev = [0]
        evp_prev = [0]
        evs_prev = [0]

        for m in range(MC_ALL):
            @block.sync
            def _(sync, m=m):
                if m >= 2:
                    sync.wait_ge(CVT.h, cvt_done[m - 2])
                wld[m] = WLDP[m % 2].inc(sync.dma_start(out=wraw[m % 2][:], in_=win_hbm[m]), 16)

            @block.gpsimd
            def _(gpsimd, m=m):
                gpsimd.wait_ge(WLDP[m % 2].h, wld[m])
                if m >= 2:
                    gpsimd.wait_ge(MM.h, mm_strip[m - 2])
                cvt_done[m] = CVT.inc(gpsimd.tensor_copy(out=wbf[m % 2][:], in_=wraw[m % 2][:]))

            @block.tensor
            def _(tensor, m=m):
                tensor.wait_ge(CVT.h, cvt_done[m])
                if m == 0:
                    tensor.wait_ge(EVV.h, ev_xevac[NT - 1])
                for tb in range(2):
                    bk = (m % 2) * 2 + tb
                    if m >= 2 and (m - 2, tb) in evd:
                        s, v = evd[(m - 2, tb)]
                        tensor.wait_ge(s.h, v)
                    last = None
                    for k in range(16):
                        last = tensor.matmul(out=pF[:, bk * 512:(bk + 1) * 512],
                                      lhsT=wbf[m % 2][:, k * 128:(k + 1) * 128],
                                      rhs=x_bf[:, k * T + tb * 512:k * T + tb * 512 + 512],
                                      start=(k == 0), stop=(k == 15))
                        if tb == 0 and 32 <= m <= 65:
                            last = tensor.matmul(out=pX[:, (m - 32) * 4:(m - 32) * 4 + 3],
                                          lhsT=wbf[m % 2][:, k * 128:(k + 1) * 128],
                                          rhs=xprevf[:, k * 4:k * 4 + 3],
                                          start=(k == 0), stop=(k == 15))
                    mme[m][tb] = MM.inc(last)
                mm_strip[m] = MM.v

            for tb in range(2):
                bk = (m % 2) * 2 + tb
                if m < MC_GATE:
                    @block.scalar
                    def _(scalar, m=m, tb=tb, bk=bk):
                        scalar.wait_ge(MM.h, mme[m][tb])
                        if spl_prev[0]:
                            scalar.wait_ge(SPL.h, spl_prev[0])
                        evs_prev[0] = EVS.inc(scalar.activation(
                            out=sgf[:], in_=pF[:, bk * 512:(bk + 1) * 512], func=AF.Silu))
                        scalar.wait_ge(EVS.h, evs_prev[0])
                        spl_prev[0] = SPL.inc(scalar.dma_start(
                            out=sg_dram[m][:, tb * 512:tb * 512 + 512], in_=sgf[:]), 16)
                    evd[(m, tb)] = (EVS, evs_prev[0])
                elif m < MC_GATE + MC_HBC:
                    @block.vector
                    def _(vector, m=m, tb=tb, bk=bk):
                        vector.wait_ge(MM.h, mme[m][tb])
                        cb = (m - 32) * HBW
                        v = EVV.inc(vector.tensor_copy(
                            out=hbc[:, cb + 3 + tb * 512:cb + 3 + tb * 512 + 512],
                            in_=pF[:, bk * 512:(bk + 1) * 512]))
                        evd[(m, tb)] = (EVV, v)
                else:
                    @block.vector
                    def _(vector, m=m, tb=tb, bk=bk):
                        vector.wait_ge(MM.h, mme[m][tb])
                        v = EVV.inc(vector.tensor_copy(
                            out=dt_raw[:, tb * 512:tb * 512 + 512],
                            in_=pF[0:64, bk * 512:(bk + 1) * 512]))
                        evd[(m, tb)] = (EVV, v)


        if stop_after < 2:

            @block.sync
            def _(sync):
                sync.wait_ge(EVV.h, EVV.v)
                sync.wait_ge(EVS.h, EVS.v)
                sync.wait_ge(EVP.h, EVP.v)
                sync.wait_ge(MM.h, MM.v)
                sync.wait_ge(SPL.h, SPL.v)
                sync.wait_ge(STO.h, STO.v)
            _close_phase_stacks()
            return nc, tap_shapes

        @block.vector
        def _(vector):
            vector.wait_ge(MM.h, mme[65][0])
            EVV.inc(vector.tensor_copy(
                out=hbc[:].rearrange("q (m w) -> q m w", m=MC_HBC)[:, :, 0:3],
                in_=pX[:, 0:136].rearrange("q (m w) -> q m w", m=MC_HBC)[:, :, 0:3]))
        ev_hbc_all = EVV.v

        tap("dt_raw", dt_raw[:], [(EVV, ev_hbc_all)])
        tap("hbc", hbc[:, 0:4 * HBW], [(EVV, ev_hbc_all)])

        @block.vector
        def _(vector):
            EVV.inc(vector.tensor_scalar_add(out=tsc64[:], in0=dt_raw[:], scalar1=dtb[:, 0:1]))
        ev_z = EVV.v

        @block.scalar
        def _(scalar):
            scalar.wait_ge(EVV.h, ev_z)
            scalar.activation(out=dt_f[:], in_=tsc64[:], func=AF.Abs)
            scalar.activation(out=dt_f[:], in_=dt_f[:], func=AF.Exp, scale=-1.0)
            EVS.inc(scalar.activation(out=dt_f[:], in_=dt_f[:], func=AF.Ln, bias=ones64[:, 0:1]))
            EVS.inc(scalar.activation(out=negA[:], in_=alog[:], func=AF.Exp))
        ev_ln1p = EVS.v

        @block.vector
        def _(vector):
            vector.wait_ge(EVS.h, ev_ln1p)
            vector.tensor_scalar_max(out=dt_raw[:], in0=tsc64[:], scalar1=0.0)
            vector.tensor_add(out=dt_f[:], in0=dt_f[:], in1=dt_raw[:])
            vector.tensor_scalar(out=dA_f[:], in0=dt_f[:], scalar1=negA[:, 0:1],
                                 scalar2=-1.0, op0=ALU.mult, op1=ALU.mult)
            EVV.inc(vector.tensor_copy(out=dA_b[:], in_=dA_f[:]))
        ev_dt = EVV.v

        @block.scalar
        def _(scalar):
            scalar.wait_ge(EVV.h, ev_dt)
            EVS.inc(scalar.activation(out=tsc64[:], in_=dt_f[:], func=AF.Ln))
        ev_lndt = EVS.v

        @block.vector
        def _(vector):
            vector.wait_ge(EVS.h, ev_lndt)
            EVV.inc(vector.tensor_copy(out=lndt_b[:], in_=tsc64[:]))

        tap("dt_f", dt_f[:], [(EVV, ev_dt)])
        tap("dA_f", dA_f[:], [(EVV, ev_dt)])
        tap("negA", negA[:], [(EVV, ev_dt)])

        esAB.close()
        esB = _mkstack()
        EB = esB.enter_context
        xs_tok = EB(sb("xs_tok", [128, NT * I], BF16, side="right"))
        b_n = EB(sb("b_n", [128, T], BF16, side="right"))
        c_n = EB(sb("c_n", [128, T], BF16, side="right"))
        b_tok = EB(sb("b_tok", [128, T], BF16, side="right"))
        dte_tok = EB(sb("dte_tok", [128, NT * 64], F32, side="right"))
        cdecN = EB(sb("cdecN", [128, NT * 64], F32, side="right"))
        carry = EB(sb("carry", [128, I], F32, side="right"))
        shb4 = EB(sb("shb4", [128, 2048], BF16, side="right"))
        accv = EB(sb("accv", [128, 512], F32, side="right"))
        accp = EB(sb("accp", [128, 512], F32, side="right"))
        accp2 = EB(sb("accp2", [128, 512], F32, side="right"))
        xsstg = EB(sb("xsstg", [128, 1024], BF16, side="right"))

        conv_done = [[0, 0] for _ in range(MC_HBC)]
        mm_xs_tr = {}
        ev_xs_evac = [0]

        for m in range(MC_HBC):
            eng_is_v = True
            for tb in range(2):
                if eng_is_v:
                    @block.vector
                    def _(vector, m=m, tb=tb):
                        vector.wait_ge(EVV.h, ev_hbc_all)
                        cb = m * HBW
                        sl = lambda k: hbc[:, cb + tb * 512 + k: cb + tb * 512 + k + 512]
                        vector.tensor_scalar_mul(out=accv[:], in0=sl(0),
                                                 scalar1=convw[:, 4 * m:4 * m + 1])
                        for k in range(1, 3):
                            vector.scalar_tensor_tensor(
                                out=accv[:], in0=sl(k), scalar=convw[:, 4 * m + k:4 * m + k + 1],
                                in1=accv[:], op0=ALU.mult, op1=ALU.add)
                        conv_done[m][tb] = EVV.inc(vector.scalar_tensor_tensor(
                            out=accv[:], in0=sl(3), scalar=convw[:, 4 * m + 3:4 * m + 4],
                            in1=accv[:], op0=ALU.mult, op1=ALU.add))
                else:
                    @block.gpsimd
                    def _(gpsimd, m=m, tb=tb):
                        gpsimd.wait_ge(EVV.h, ev_hbc_all)
                        cb = m * HBW
                        sl = lambda k: hbc[:, cb + tb * 512 + k: cb + tb * 512 + k + 512]
                        gpsimd.tensor_scalar_mul(out=accp[:], in0=sl(0),
                                                 scalar1=convw[:, 4 * m:4 * m + 1])
                        last = None
                        for k in range(1, 4):
                            gpsimd.tensor_scalar_mul(out=accp2[:], in0=sl(k),
                                                     scalar1=convw[:, 4 * m + k:4 * m + k + 1])
                            last = gpsimd.tensor_add(out=accp[:], in0=accp[:], in1=accp2[:])
                        conv_done[m][tb] = EVP.inc(last)

                @block.scalar
                def _(scalar, m=m, tb=tb, eng_is_v=eng_is_v):
                    sem = EVV if eng_is_v else EVP
                    scalar.wait_ge(sem.h, conv_done[m][tb])
                    prev_tr = [mm_xs_tr[mm] for mm in mm_xs_tr if mm < m]
                    if m < 32 and prev_tr:
                        scalar.wait_ge(MM.h, prev_tr[-1])
                    acc = accv if eng_is_v else accp
                    if m < 32:
                        dst = xsstg[:, tb * 512:tb * 512 + 512]
                    elif m == 32:
                        dst = b_n[:, tb * 512:tb * 512 + 512]
                    else:
                        dst = c_n[:, tb * 512:tb * 512 + 512]
                    EVS.inc(scalar.activation(out=dst, in_=acc[:], func=AF.Silu,
                                              bias=convb[:, m:m + 1]))
                s_after = EVS.v
                if eng_is_v:
                    @block.vector
                    def _(vector, v=s_after):
                        vector.wait_ge(EVS.h, v)
                else:
                    @block.gpsimd
                    def _(gpsimd, v=s_after):
                        gpsimd.wait_ge(EVS.h, v)

            if m <= 32:
                src = xsstg if m < 32 else b_n

                @block.tensor
                def _(tensor, m=m, src=src):
                    tensor.wait_ge(EVS.h, EVS.v)
                    tensor.wait_ge(EVV.h, ev_xs_evac[0] if ev_xs_evac[0] else ev_xevac[NT - 1])
                    last = None
                    for c in range(8):
                        last = tensor.transpose(out=pTB[:, c * 128:(c + 1) * 128],
                                         in_=src[:, c * 128:(c + 1) * 128],
                                         identity=id128b[:])
                    mm_xs_tr[m] = MM.inc(last)

                @block.vector
                def _(vector, m=m):
                    vector.wait_ge(MM.h, mm_xs_tr[m])
                    if m < 32:
                        dst = xs_tok[:].rearrange("q (c f) -> q c f", c=NT)[:, :, m * 128:(m + 1) * 128]
                    else:
                        dst = b_tok[:].rearrange("q (c f) -> q c f", c=NT)
                    ev_xs_evac[0] = EVV.inc(vector.tensor_copy(
                        out=dst, in_=pTB[:, 0:1024].rearrange("q (c t) -> q c t", c=8)))

        ev_convB = EVV.v
        tap("xs_tok", xs_tok[:, 0:I], [(EVV, ev_convB)])
        tap("b_n", b_n[:], [(EVV, ev_convB)])
        tap("c_n", c_n[:], [(EVV, ev_convB)])
        tap("b_tok", b_tok[:], [(EVV, ev_convB)])

        esHBC.close()


        if stop_after < 3:
            @block.sync
            def _(sync):
                sync.wait_ge(EVV.h, EVV.v)
                sync.wait_ge(EVS.h, EVS.v)
                sync.wait_ge(EVP.h, EVP.v)
                sync.wait_ge(MM.h, MM.v)
                sync.wait_ge(SPL.h, SPL.v)
                sync.wait_ge(STO.h, STO.v)
            _close_phase_stacks()
            return nc, tap_shapes

        @block.vector
        def _(vector):
            vector.wait_ge(EVV.h, ev_convB)
            vector.memset(carry[:], 0.0)
            EVV.inc(vector.tensor_reduce(out=alsum[:], in_=dA_f[:],
                                         axis=mybir.AxisListType.X, op=ALU.add))

        ev_cdec_prev = [0]
        ev_carry_upd = [0]
        for c in range(NT):
            cs = slice(c * 128, (c + 1) * 128)

            @block.vector
            def _(vector, c=c, cs=cs):
                EVV.inc(vector.tensor_tensor_scan(
                    out=acumT[:, cs], data0=dA_f[:, cs], data1=dA_f[:, cs],
                    initial=0.0, op0=ALU.add, op1=ALU.bypass))
            ev_acum = EVV.v

            @block.scalar
            def _(scalar, c=c, cs=cs):
                scalar.wait_ge(EVV.h, ev_acum)
                EVS.inc(scalar.activation(out=tsc64[:, cs], in_=acumT[:, cs], func=AF.Exp,
                                          scale=-1.0, bias=acumT[:, c * 128 + 127:c * 128 + 128]))
            ev_dte1 = EVS.v

            @block.vector
            def _(vector, c=c, cs=cs):
                vector.wait_ge(EVS.h, ev_dte1)
                EVV.inc(vector.tensor_mul(out=tsc64[:, cs], in0=tsc64[:, cs], in1=dt_f[:, cs]))
            ev_dte2 = EVV.v

            @block.tensor
            def _(tensor, c=c, cs=cs):
                tensor.wait_ge(EVV.h, ev_dte2)
                if c > 0:
                    tensor.wait_ge(EVS.h, ev_cdec_prev[0])
                tensor.transpose(out=pF[:, 2048:2112], in_=tsc64[:, cs], identity=id64f[:])
                MM.inc(tensor.matmul(out=pF[:, 2112:2176],
                              lhsT=acumT[:, c * 128 + 127:c * 128 + 128].broadcast_to([64, 128]),
                              rhs=id64f[:], start=True, stop=True))
            mm_c1 = MM.v

            @block.vector
            def _(vector, c=c):
                vector.wait_ge(MM.h, mm_c1)
                EVV.inc(vector.tensor_copy(out=dte_tok[:, c * 64:(c + 1) * 64],
                                           in_=pF[:, 2048:2112]))
            ev_dtetok = EVV.v

            @block.scalar
            def _(scalar, c=c):
                scalar.wait_ge(MM.h, mm_c1)
                ev_cdec_prev[0] = EVS.inc(scalar.activation(
                    out=cdecN[:, c * 64:(c + 1) * 64], in_=pF[:, 2112:2176], func=AF.Exp))

            for hh in range(2):
                @block.vector
                def _(vector, c=c, hh=hh):
                    vector.wait_ge(EVV.h, ev_dtetok)
                    vector.wait_ge(EVS.h, ev_cdec_prev[0])
                    vector.wait_ge(MM.h, MM.v)
                    EVV.inc(vector.tensor_tensor(
                        out=shb4[:].rearrange("q (h p2) -> q h p2", h=32),
                        in0=xs_tok[:, c * I + hh * 2048:c * I + hh * 2048 + 2048]
                            .rearrange("q (h p2) -> q h p2", h=32),
                        in1=dte_tok[:, c * 64 + hh * 32:c * 64 + hh * 32 + 32].unsqueeze(2)
                            .broadcast_to([128, 32, 64]),
                        op=ALU.mult))
                ev_dxs = EVV.v

                @block.tensor
                def _(tensor, c=c, hh=hh):
                    tensor.wait_ge(EVV.h, ev_dxs)
                    if ev_carry_upd[0]:
                        tensor.wait_ge(EVV.h, ev_carry_upd[0])
                    last = None
                    for j in range(4):
                        last = tensor.matmul(out=pF[:, j * 512:(j + 1) * 512],
                                      lhsT=b_tok[:, c * 128:(c + 1) * 128],
                                      rhs=shb4[:, j * 512:(j + 1) * 512],
                                      start=True, stop=True)
                    MM.inc(last)
                mm_st = MM.v

                @block.vector
                def _(vector, c=c, hh=hh):
                    vector.wait_ge(MM.h, mm_st)
                    ch = carry[:, hh * 2048:hh * 2048 + 2048]
                    vector.tensor_tensor(
                        out=ch.rearrange("q (h p2) -> q h p2", h=32),
                        in0=ch.rearrange("q (h p2) -> q h p2", h=32),
                        in1=cdecN[:, c * 64 + hh * 32:c * 64 + hh * 32 + 32].unsqueeze(2)
                            .broadcast_to([128, 32, 64]),
                        op=ALU.mult)
                    ev_carry_upd[0] = EVV.inc(vector.tensor_add(out=ch, in0=ch, in1=pF[:, 0:2048]))

        tap("acumT", acumT[:], [(EVV, EVV.v)])
        tap("dte_tok", dte_tok[:], [(EVV, EVV.v)])
        tap("cdecN", cdecN[:], [(EVV, EVV.v)])
        tap("S_core", carry[:], [(EVV, EVV.v)])


        if stop_after < 4:
            @block.sync
            def _(sync):
                sync.wait_ge(EVV.h, EVV.v)
                sync.wait_ge(EVS.h, EVS.v)
                sync.wait_ge(EVP.h, EVP.v)
                sync.wait_ge(MM.h, MM.v)
                sync.wait_ge(SPL.h, SPL.v)
                sync.wait_ge(STO.h, STO.v)
            _close_phase_stacks()
            return nc, tap_shapes

        ag_in = nc.dram_tensor("ag_in", [128, 4160], F32, kind="Internal").ap()
        ag_out = nc.dram_tensor("ag_out", [128 * NCORES, 4160], F32, kind="Internal",
                                addr_space="Shared").ap()
        @block.tensor
        def _(tensor):
            tensor.wait_ge(EVV.h, EVV.v)
            tensor.wait_ge(EVS.h, ev_cdec_prev[0])
            MM.inc(tensor.matmul(out=pF[:, 2048:2112],
                          lhsT=alsum[:, 0:1].broadcast_to([64, 128]),
                          rhs=id64f[:], start=True, stop=True))
        mm_dcore = MM.v

        @block.scalar
        def _(scalar):
            scalar.wait_ge(MM.h, mm_dcore)
            EVS.inc(scalar.activation(out=dcrN[:], in_=pF[:, 2048:2112], func=AF.Exp))
        ev_dcore = EVS.v

        tap("dcoreN", dcrN[:], [(EVS, ev_dcore)])

        @block.scalar
        def _(scalar):
            scalar.wait_ge(EVS.h, ev_dcore)
            SPL.inc(scalar.dma_start(out=ag_in[:, 0:4096], in_=carry[:]), 16)
            SPL.inc(scalar.dma_start(out=ag_in[:, 4096:4160], in_=dcrN[:]), 16)
        spl_ag = SPL.v

        @block.gpsimd
        def _(gpsimd):
            gpsimd.wait_ge(SPL.h, spl_ag)
            CC.inc(gpsimd.collective_compute(
                "AllGather", ALU.bypass,
                replica_groups=[list(range(NCORES))],
                ins=[ag_in.opt()], outs=[ag_out.opt()]))
        cc_done = CC.v

        esDT.close()
        esH = _mkstack()
        hornS = esH.enter_context(sb("hornS", [128, 4160], F32, side="right"))

        ev_horn = [0] * NCORES
        for j in range(NCORES):
            @block.gpsimd
            def _(gpsimd, j=j):
                if j == 0:
                    gpsimd.wait_ge(CC.h, cc_done)
                else:
                    gpsimd.wait_ge(EVV.h, ev_horn[j - 1])
                GDM.inc(gpsimd.dma_start(
                    out=hornS[:], in_=ag_out[j * 128:(j + 1) * 128, :]), 16)
            gdm_h = GDM.v

            @block.vector
            def _(vector, j=j, gdm_h=gdm_h):
                vector.wait_ge(GDM.h, gdm_h)
                if j == 0:
                    vector.wait_ge(SPL.h, spl_ag)
                    vector.memset(carry[:], 0.0)
                vector.tensor_scalar(out=dcrN[:], in0=hornS[:, 4096:4160],
                                     scalar1=emask[:, j:j + 1], scalar2=emask[:, 8 + j:9 + j],
                                     op0=ALU.mult, op1=ALU.add)
                vector.tensor_tensor(
                    out=carry[:].rearrange("q (h p2) -> q h p2", h=64),
                    in0=carry[:].rearrange("q (h p2) -> q h p2", h=64),
                    in1=dcrN[:].unsqueeze(2).broadcast_to([128, 64, 64]),
                    op=ALU.mult)
                ev_horn[j] = EVV.inc(vector.scalar_tensor_tensor(
                    out=carry[:], in0=hornS[:, 0:4096], scalar=emask[:, j:j + 1],
                    in1=carry[:], op0=ALU.mult, op1=ALU.add))
        esH.close()

        tap("carry_in", carry[:], [(EVV, EVV.v)])


        if stop_after < 5:
            @block.sync
            def _(sync):
                sync.wait_ge(EVV.h, EVV.v)
                sync.wait_ge(EVS.h, EVS.v)
                sync.wait_ge(EVP.h, EVP.v)
                sync.wait_ge(MM.h, MM.v)
                sync.wait_ge(SPL.h, SPL.v)
                sync.wait_ge(STO.h, STO.v)
            _close_phase_stacks()
            return nc, tap_shapes

        esYG = _mkstack()
        yg_f = esYG.enter_context(sb("yg_f", [128, 32 * T], BF16, side="left"))

        esC2 = _mkstack()
        EC = esC2.enter_context
        scrA = EC(sb("scrA", [128, 4096], BF16, side="right"))
        sgtok = EC(sb("sgtok", [128, 2048], BF16, side="right"))
        sgstage = EC(sb("sgstage", [128, 2048], BF16, side="right"))
        ygh0 = EC(sb("ygh0", [128, 2048], BF16, side="right"))
        ytmp = EC(sb("ytmp", [128, 512], F32, side="right"))
        datok = EC(sb("datok", [128, 64], BF16, side="right"))
        lntok = EC(sb("lntok", [128, 64], BF16, side="right"))
        eatok = EC(sb("eatok", [128, 64], F32, side="right"))
        ea64 = EC(sb("ea64", [64, 128], F32, side="right"))
        cbm = EC(sb("cbm", [128, 128], BF16, side="right"))
        ssqc = EC(sb("ssqc", [128, 8], F32, side="right"))
        ssum = EC(sb("ssum", [128, 1], F32, side="right"))

        ev_ptb_free = [0]
        mm_scrA_free = [0]
        ev_pf_low_free = [ev_carry_upd[0]]
        ev_small_evac = [0]

        for c in range(NT):
            cs = slice(c * 128, (c + 1) * 128)

            @block.scalar
            def _(scalar, c=c, cs=cs):
                if c == 0:
                    scalar.wait_ge(EVV.h, ev_horn[NCORES - 1])
                EVS.inc(scalar.activation(out=ea64[:], in_=acumT[:, cs], func=AF.Exp))
            ev_ea = EVS.v

            @block.tensor
            def _(tensor, c=c, cs=cs):
                tensor.wait_ge(EVS.h, ev_ea)
                if ev_small_evac[0]:
                    tensor.wait_ge(EVV.h, ev_small_evac[0])
                tensor.wait_ge(EVV.h, ev_ptb_free[0] if ev_ptb_free[0] else ev_xs_evac[0])
                tensor.transpose(out=pTB[:, 0:64], in_=dA_b[:, cs], identity=id64b[:])
                tensor.transpose(out=pTB[:, 64:128], in_=lndt_b[:, cs], identity=id64b[:])
                tensor.transpose(out=pF[:, 2048:2112], in_=ea64[:], identity=id64f[:])
                MM.inc(tensor.matmul(out=pF[:, 2176:2304], lhsT=b_n[:, cs], rhs=c_n[:, cs],
                              start=True, stop=True))
            mm_small = MM.v

            @block.vector
            def _(vector, c=c):
                vector.wait_ge(MM.h, mm_small)
                vector.tensor_copy(out=datok[:], in_=pTB[:, 0:64])
                vector.tensor_copy(out=lntok[:], in_=pTB[:, 64:128])
                vector.tensor_copy(out=eatok[:], in_=pF[:, 2048:2112])
                v = EVV.inc(vector.tensor_tensor(out=cbm[:], in0=pF[:, 2176:2304],
                                                 in1=trige[:], op=ALU.mult))
                ev_small_evac[0] = v
                ev_ptb_free[0] = v

            for hh in range(2):
                hoff = hh * 2048
                @block.sync
                def _(sync, c=c, hh=hh):
                    if ev_ptb_free[0]:
                        sync.wait_ge(EVV.h, ev_ptb_free[0])
                    src = sg_dram[16 * hh:16 * hh + 16, :, c * 128:(c + 1) * 128]
                    LD.inc(sync.dma_start(
                        out=sgstage[:].rearrange("q (f l) -> q f l", f=16),
                        in_=src.rearrange("m q t -> q m t")), 16)
                ld_sg = LD.v

                @block.tensor
                def _(tensor, c=c, hh=hh):
                    tensor.wait_ge(LD.h, ld_sg)
                    tensor.wait_ge(EVV.h, ev_ptb_free[0])
                    last = None
                    for ft in range(16):
                        last = tensor.transpose(out=pTB[:, ft * 128:(ft + 1) * 128],
                                         in_=sgstage[:, ft * 128:(ft + 1) * 128],
                                         identity=id128b[:])
                    MM.inc(last)
                mm_sgtr = MM.v

                @block.vector
                def _(vector, c=c, hh=hh, mm_sgtr=mm_sgtr):
                    vector.wait_ge(MM.h, mm_sgtr)
                    v = EVV.inc(vector.tensor_copy(out=sgtok[:], in_=pTB[:]))
                    ev_ptb_free[0] = v
                ev_sgtok = EVV.v

                @block.vector
                def _(vector, c=c, hh=hh):
                    vector.wait_ge(EVV.h, ev_small_evac[0])
                    if mm_scrA_free[0]:
                        vector.wait_ge(MM.h, mm_scrA_free[0])
                    EVV.inc(vector.tensor_tensor(
                        out=scrA[:].rearrange("q (h t) -> q h t", h=32),
                        in0=datok[:, hh * 32:hh * 32 + 32].unsqueeze(2).broadcast_to([128, 32, 128]),
                        in1=trile[:].unsqueeze(1).broadcast_to([128, 32, 128]),
                        op=ALU.mult))
                ev_rhs1 = EVV.v
                if c == 0 and hh == 0:
                    tap("dbg_rhs1", scrA[:], [(EVV, ev_rhs1)])
                    tap("dbg_datok", datok[:], [(EVV, ev_rhs1)])
                    tap("dbg_lntok", lntok[:], [(EVV, ev_rhs1)])
                    tap("dbg_eatok", eatok[:], [(EVV, ev_rhs1)])
                    tap("dbg_cbm", cbm[:], [(EVV, ev_rhs1)])
                    tap("dbg_sgtok", sgtok[:], [(EVV, ev_sgtok)])

                ev_exp_q = [0, 0]
                for q2 in range(2):
                    @block.tensor
                    def _(tensor, c=c, hh=hh, q2=q2):
                        tensor.wait_ge(EVV.h, ev_rhs1)
                        if q2 == 1:
                            tensor.wait_ge(EVS.h, ev_exp_q[0])
                        else:
                            tensor.wait_ge(EVV.h, ev_pf_low_free[0])
                        for nb in range(4):
                            col = q2 * 2048 + nb * 512
                            hsel = hh * 32 + q2 * 16 + nb * 4
                            tensor.matmul(out=pF[:, nb * 512:(nb + 1) * 512],
                                          lhsT=trigt[:], rhs=scrA[:, col:col + 512],
                                          start=True, stop=False)
                            last = tensor.matmul(out=pF[:, nb * 512:(nb + 1) * 512].rearrange(
                                              "q (h t) -> q h t", h=4),
                                          lhsT=id128b[:],
                                          rhs=lntok[:, hsel:hsel + 4].unsqueeze(2)
                                              .broadcast_to([128, 4, 128]),
                                          start=False, stop=True)
                        MM.inc(last)
                    mm_arg = MM.v

                    @block.scalar
                    def _(scalar, c=c, hh=hh, q2=q2, mm_arg=mm_arg):
                        scalar.wait_ge(MM.h, mm_arg)
                        ev_exp_q[q2] = EVS.inc(scalar.activation(
                            out=scrA[:, q2 * 2048:q2 * 2048 + 2048],
                            in_=pF[:, 0:2048], func=AF.Exp))

                @block.vector
                def _(vector, c=c, hh=hh):
                    vector.wait_ge(EVS.h, ev_exp_q[1])
                    EVV.inc(vector.tensor_tensor(
                        out=scrA[:].rearrange("q (h t) -> q h t", h=32),
                        in0=scrA[:].rearrange("q (h t) -> q h t", h=32),
                        in1=cbm[:].unsqueeze(1).broadcast_to([128, 32, 128]),
                        op=ALU.mult))
                ev_m = EVV.v
                if c == 0 and hh == 0:
                    tap("dbg_M", scrA[:], [(EVV, ev_m)])

                @block.vector
                def _(vector, c=c, hh=hh):
                    EVV.inc(vector.tensor_copy(out=shb4[:], in_=carry[:, hoff:hoff + 2048]))
                ev_cbf = EVV.v

                @block.tensor
                def _(tensor, c=c, hh=hh, cs=cs):
                    tensor.wait_ge(EVV.h, ev_cbf)
                    tensor.wait_ge(EVS.h, ev_exp_q[1])
                    last = None
                    for o in range(4):
                        last = tensor.matmul(out=pF[:, o * 512:(o + 1) * 512],
                                      lhsT=c_n[:, cs], rhs=shb4[:, o * 512:(o + 1) * 512],
                                      start=True, stop=True)
                    MM.inc(last)
                mm_y2 = MM.v

                @block.vector
                def _(vector, c=c, hh=hh, mm_y2=mm_y2):
                    vector.wait_ge(MM.h, mm_y2)
                    EVV.inc(vector.tensor_tensor(
                        out=shb4[:].rearrange("q (h p2) -> q h p2", h=32),
                        in0=xs_tok[:, c * I + hoff:c * I + hoff + 2048]
                            .rearrange("q (h p2) -> q h p2", h=32),
                        in1=d_n[:, hh * 32:hh * 32 + 32].unsqueeze(2).broadcast_to([128, 32, 64]),
                        op=ALU.mult))
                ev_xsd = EVV.v

                ev_oct = [0]
                for o in range(4):
                    @block.tensor
                    def _(tensor, c=c, hh=hh, o=o):
                        tensor.wait_ge(EVV.h, ev_m)
                        tensor.wait_ge(EVV.h, ev_xsd)
                        tensor.wait_ge(EVV.h, ev_sgtok)
                        if ev_oct[0]:
                            tensor.wait_ge(EVV.h, ev_oct[0])
                        for jh in range(8):
                            h8 = o * 8 + jh
                            tensor.matmul(out=pF[:, 2048 + jh * 64:2048 + (jh + 1) * 64],
                                          lhsT=scrA[:, h8 * 128:(h8 + 1) * 128],
                                          rhs=xs_tok[:, c * I + hoff + h8 * 64:
                                                     c * I + hoff + h8 * 64 + 64],
                                          start=True, stop=False)
                            last = tensor.matmul(out=pF[:, 2048 + jh * 64:2048 + (jh + 1) * 64],
                                          lhsT=id128b[:],
                                          rhs=shb4[:, h8 * 64:(h8 + 1) * 64],
                                          start=False, stop=True)
                        MM.inc(last)
                    mm_y1 = MM.v
                    mm_scrA_free[0] = mm_y1

                    @block.vector
                    def _(vector, c=c, hh=hh, o=o, mm_y1=mm_y1):
                        vector.wait_ge(MM.h, mm_y1)
                        vector.tensor_tensor(
                            out=ytmp[:].rearrange("q (h p2) -> q h p2", h=8),
                            in0=pF[:, o * 512:(o + 1) * 512].rearrange("q (h p2) -> q h p2", h=8),
                            in1=eatok[:, hh * 32 + o * 8:hh * 32 + o * 8 + 8].unsqueeze(2)
                                .broadcast_to([128, 8, 64]),
                            op=ALU.mult)
                        vector.tensor_add(out=ytmp[:], in0=ytmp[:], in1=pF[:, 2048:2560])
                        vector.tensor_tensor(out=scrA[:, o * 512:(o + 1) * 512],
                                             in0=ytmp[:], in1=sgtok[:, o * 512:(o + 1) * 512],
                                             op=ALU.mult)
                        ev_oct[0] = EVV.inc(vector.scalar_tensor_tensor(
                            out=ytmp[:], in0=scrA[:, o * 512:(o + 1) * 512], scalar=1.0,
                            in1=scrA[:, o * 512:(o + 1) * 512], op0=ALU.mult, op1=ALU.mult,
                            accum_out=ssqc[:, hh * 4 + o:hh * 4 + o + 1]))

                if hh == 0:
                    @block.vector
                    def _(vector, c=c):
                        EVV.inc(vector.tensor_copy(out=ygh0[:], in_=scrA[:, 0:2048]))
                    if c == 0:
                        tap("dbg_ygh0", ygh0[:], [(EVV, EVV.v)])
                ev_ygpark = EVV.v

                @block.vector
                def _(vector, c=c, hh=hh):
                    EVV.inc(vector.tensor_tensor(
                        out=shb4[:].rearrange("q (h p2) -> q h p2", h=32),
                        in0=xs_tok[:, c * I + hoff:c * I + hoff + 2048]
                            .rearrange("q (h p2) -> q h p2", h=32),
                        in1=dte_tok[:, c * 64 + hh * 32:c * 64 + hh * 32 + 32].unsqueeze(2)
                            .broadcast_to([128, 32, 64]),
                        op=ALU.mult))
                ev_dxs2 = EVV.v

                @block.tensor
                def _(tensor, c=c, hh=hh, cs=cs):
                    tensor.wait_ge(EVV.h, ev_dxs2)
                    last = None
                    for j in range(4):
                        last = tensor.matmul(out=pF[:, j * 512:(j + 1) * 512],
                                      lhsT=b_tok[:, cs],
                                      rhs=shb4[:, j * 512:(j + 1) * 512],
                                      start=True, stop=True)
                    MM.inc(last)
                mm_st2 = MM.v

                @block.vector
                def _(vector, c=c, hh=hh, mm_st2=mm_st2):
                    vector.wait_ge(MM.h, mm_st2)
                    ch = carry[:, hoff:hoff + 2048]
                    vector.tensor_tensor(
                        out=ch.rearrange("q (h p2) -> q h p2", h=32),
                        in0=ch.rearrange("q (h p2) -> q h p2", h=32),
                        in1=cdecN[:, c * 64 + hh * 32:c * 64 + hh * 32 + 32].unsqueeze(2)
                            .broadcast_to([128, 32, 64]),
                        op=ALU.mult)
                    v = EVV.inc(vector.tensor_add(out=ch, in0=ch, in1=pF[:, 0:2048]))
                    ev_pf_low_free[0] = v

            if c == 0:
                tap("dbg_ssqc", ssqc[:], [(EVV, EVV.v)])
            @block.gpsimd
            def _(gpsimd, c=c, evoct=ev_oct[0]):
                gpsimd.wait_ge(EVV.h, evoct)
                gpsimd.tensor_add(out=ssum[:], in0=ssqc[:, 0:1], in1=ssqc[:, 1:2])
                for qq in range(2, 8):
                    gpsimd.tensor_add(out=ssum[:], in0=ssum[:], in1=ssqc[:, qq:qq + 1])
                EVP.inc(gpsimd.tensor_scalar(out=ssum[:], in0=ssum[:], scalar1=1.0 / I,
                                             scalar2=EPS, op0=ALU.mult, op1=ALU.add))
            ev_var = EVP.v
            if c == 0:
                tap("dbg_ssum", ssum[:], [(EVP, ev_var)])

            @block.scalar
            def _(scalar, c=c, ev_var=ev_var):
                scalar.wait_ge(EVP.h, ev_var)
                scalar.activation(out=rinv8[:, c:c + 1], in_=ssum[:], func=AF.Ln)
                EVS.inc(scalar.activation(out=rinv8[:, c:c + 1], in_=rinv8[:, c:c + 1],
                                          func=AF.Exp, scale=-0.5))
            ev_rinv = EVS.v

            for hh in range(2):
                srcbuf = ygh0 if hh == 0 else scrA

                @block.vector
                def _(vector, c=c, hh=hh, srcbuf=srcbuf, ev_rinv=ev_rinv):
                    vector.wait_ge(EVS.h, ev_rinv)
                    EVV.inc(vector.tensor_scalar_mul(
                        out=srcbuf[:, 0:2048], in0=srcbuf[:, 0:2048],
                        scalar1=rinv8[:, c:c + 1]))
                ev_rapp = EVV.v

                @block.tensor
                def _(tensor, c=c, hh=hh, srcbuf=srcbuf, ev_rapp=ev_rapp):
                    tensor.wait_ge(EVV.h, ev_rapp)
                    tensor.wait_ge(EVV.h, ev_ptb_free[0])
                    last = None
                    for ft in range(16):
                        last = tensor.transpose(out=pTB[:, ft * 128:(ft + 1) * 128],
                                         in_=srcbuf[:, ft * 128:(ft + 1) * 128],
                                         identity=id128b[:])
                    MM.inc(last)
                mm_ygtr = MM.v
                if hh == 1:
                    mm_scrA_free[0] = mm_ygtr

                @block.vector
                def _(vector, c=c, hh=hh, mm_ygtr=mm_ygtr):
                    vector.wait_ge(MM.h, mm_ygtr)
                    last = None
                    for ft in range(16):
                        fb = hh * 16 + ft
                        last = vector.tensor_scalar_mul(
                            out=yg_f[:, fb * T + c * 128:fb * T + (c + 1) * 128],
                            in0=pTB[:, ft * 128:(ft + 1) * 128],
                            scalar1=normw[:, fb:fb + 1])
                    ev_ptb_free[0] = EVV.inc(last)

        ev_c2 = EVV.v
        tap("yg_f", yg_f[:, 0:T], [(EVV, ev_c2)])
        tap("rinv8", rinv8[:], [(EVV, ev_c2)])

        esC2.close()
        esB.close()


        if stop_after < 6:
            @block.sync
            def _(sync):
                sync.wait_ge(EVV.h, EVV.v)
                sync.wait_ge(EVS.h, EVS.v)
                sync.wait_ge(EVP.h, EVP.v)
                sync.wait_ge(MM.h, MM.v)
                sync.wait_ge(SPL.h, SPL.v)
                sync.wait_ge(STO.h, STO.v)
            _close_phase_stacks()
            return nc, tap_shapes

        esD = _mkstack()
        ED = esD.enter_context
        z1 = ED(sb("z1", [128, NT * HID], F32, side="right"))
        wraw2 = [ED(sb(f"wraw2{i}", [128, 4096], F32, side="right")) for i in range(2)]
        wbf2 = [ED(sb(f"wbf2{i}", [128, 4096], BF16, side="right")) for i in range(2)]
        ztmp = ED(sb("ztmp", [128, 512], F32, side="right"))
        lnscr = ED(sb("lnscr", [128, 2048], F32, side="right"))
        x2bstg = ED(sb("x2bstg", [128, 2048], BF16, side="right"))

        @block.sync
        def _(sync, e1=EVV.v, e2=EVS.v, e3=EVP.v, e4=MM.v):
            sync.wait_ge(EVV.h, e1)
            sync.wait_ge(EVS.h, e2)
            sync.wait_ge(EVP.h, e3)
            sync.wait_ge(MM.h, e4)
            for i in range(NT):
                LD.inc(sync.dma_start(out=z1[:, i * HID:(i + 1) * HID],
                                      in_=x_hbm[i * 128:(i + 1) * 128, :]), 16)
        ld_z1 = LD.v

        if "dbg_wout_hbm" in taps:
            th_w = nc.dram_tensor("tap_dbg_wout_hbm", [128, 4096], F32, kind="ExternalOutput")
            tap_shapes["dbg_wout_hbm"] = (128, 4096)

            @block.sync
            def _(sync):
                STO.inc(sync.dma_start(out=th_w.ap(), in_=wout_hbm[1]), 16)

        wldD = [0] * 16
        cvtD = [0] * 16
        mmD = {}
        mm_stripD = [0] * 16
        evdD = {}
        ev_ztr_evac = [0]
        mm_ztrD = [0]
        for m in range(16):
            @block.sync
            def _(sync, m=m):
                if m >= 2:
                    sync.wait_ge(CVT.h, cvtD[m - 2])
                wldD[m] = WLDP[m % 2].inc(sync.dma_start(out=wraw2[m % 2][:], in_=wout_hbm[m]), 16)

            @block.gpsimd
            def _(gpsimd, m=m):
                gpsimd.wait_ge(WLDP[m % 2].h, wldD[m])
                if m >= 2:
                    gpsimd.wait_ge(MM.h, mm_stripD[m - 2])
                cvtD[m] = CVT.inc(gpsimd.tensor_copy(out=wbf2[m % 2][:], in_=wraw2[m % 2][:]))


            @block.tensor
            def _(tensor, m=m):
                tensor.wait_ge(CVT.h, cvtD[m])
                if m == 0:
                    tensor.wait_ge(EVV.h, ev_c2)
                for tb in range(2):
                    bk = (m % 2) * 2 + tb
                    if (m - 2, tb) in evdD:
                        s, v = evdD[(m - 2, tb)]
                        tensor.wait_ge(s.h, v)
                    last = None
                    for k in range(32):
                        last = tensor.matmul(out=pF[:, bk * 512:(bk + 1) * 512],
                                      lhsT=wbf2[m % 2][:, k * 128:(k + 1) * 128],
                                      rhs=yg_f[:, k * T + tb * 512:k * T + tb * 512 + 512],
                                      start=(k == 0), stop=(k == 31))
                    mmD[(m, tb)] = MM.inc(last)
                mm_stripD[m] = MM.v

            for tb in range(2):
                bk = (m % 2) * 2 + tb

                @block.vector
                def _(vector, m=m, tb=tb, bk=bk):
                    vector.wait_ge(MM.h, mmD[(m, tb)])
                    if mm_ztrD[0]:
                        vector.wait_ge(MM.h, mm_ztrD[0])
                    v = EVV.inc(vector.tensor_copy(out=ztmp[:], in_=pF[:, bk * 512:(bk + 1) * 512]))
                    evdD[(m, tb)] = (EVV, v)
                ev_zt = EVV.v
                if m == 1 and tb == 0:
                    tap("dbg_ztmp", ztmp[:], [(EVV, ev_zt)])

                @block.tensor
                def _(tensor, m=m, tb=tb, ev_zt=ev_zt):
                    tensor.wait_ge(EVV.h, ev_zt)
                    if ev_ztr_evac[0]:
                        tensor.wait_ge(EVV.h, ev_ztr_evac[0])
                    last = None
                    for q in range(4):
                        last = tensor.transpose(out=pF[:, 2048 + q * 128:2048 + (q + 1) * 128],
                                         in_=ztmp[:, q * 128:(q + 1) * 128],
                                         identity=id128f[:])
                    MM.inc(last)
                mm_ztrD[0] = MM.v
                mm_ztr_here = MM.v

                @block.vector
                def _(vector, m=m, tb=tb, mm_ztr_here=mm_ztr_here):
                    vector.wait_ge(MM.h, mm_ztr_here)
                    vector.wait_ge(LD.h, ld_z1)
                    last = None
                    for q in range(4):
                        tile = tb * 4 + q
                        last = vector.tensor_add(
                            out=z1[:, tile * HID + m * 128:tile * HID + (m + 1) * 128],
                            in0=pF[:, 2048 + q * 128:2048 + (q + 1) * 128],
                            in1=z1[:, tile * HID + m * 128:tile * HID + (m + 1) * 128])
                    ev_ztr_evac[0] = EVV.inc(last)

        ev_wout_done = EVV.v
        tap("dbg_z1", z1[:, 0:HID], [(EVV, ev_wout_done)])
        tap("dbg_wraw2b", wraw2[0][:], [(EVV, ev_wout_done)])
        tap("dbg_wbf2b", wbf2[0][:], [(EVV, ev_wout_done)])
        esYG.close()

        @block.vector
        def _(vector):
            vector.wait_ge(EVV.h, ev_wout_done)
            for tt in range(NT):
                zt = z1[:, tt * HID:(tt + 1) * HID]
                vector.scalar_tensor_tensor(out=lnscr[:], in0=zt, scalar=1.0, in1=zt,
                                            op0=ALU.mult, op1=ALU.mult,
                                            accum_out=ssq16[:, tt:tt + 1])
                vector.tensor_scalar(out=lnscr[:], in0=zt, scalar1=0.0, scalar2=0.0,
                                     op0=ALU.add, op1=ALU.add,
                                     accum_out=sums16[:, tt:tt + 1])
            EVV.inc(vector.engine_nop())
        ev_ln1acc = EVV.v

        @block.gpsimd
        def _(gpsimd):
            gpsimd.wait_ge(EVV.h, ev_ln1acc)
            gpsimd.tensor_scalar_mul(out=mucol[:, 0:8], in0=sums16[:, 0:8], scalar1=1.0 / HID)
            gpsimd.tensor_scalar_mul(out=rstd[:, 0:8], in0=ssq16[:, 0:8], scalar1=1.0 / HID)
            gpsimd.tensor_mul(out=sums16[:, 0:8], in0=mucol[:, 0:8], in1=mucol[:, 0:8])
            gpsimd.tensor_sub(out=rstd[:, 0:8], in0=rstd[:, 0:8], in1=sums16[:, 0:8])
            EVP.inc(gpsimd.tensor_scalar_add(out=rstd[:, 0:8], in0=rstd[:, 0:8], scalar1=EPS))
        ev_ln1v = EVP.v

        @block.scalar
        def _(scalar):
            scalar.wait_ge(EVP.h, ev_ln1v)
            scalar.activation(out=rstd[:, 0:8], in_=rstd[:, 0:8], func=AF.Ln)
            EVS.inc(scalar.activation(out=rstd[:, 0:8], in_=rstd[:, 0:8], func=AF.Exp,
                                      scale=-0.5))
        ev_ln1s = EVS.v

        @block.vector
        def _(vector):
            vector.wait_ge(EVS.h, ev_ln1s)
            last = None
            for tt in range(NT):
                zt = z1[:, tt * HID:(tt + 1) * HID]
                last = vector.tensor_scalar(out=zt, in0=zt, scalar1=mucol[:, tt:tt + 1],
                                            scalar2=rstd[:, tt:tt + 1],
                                            op0=ALU.subtract, op1=ALU.mult)
            EVV.inc(last)
        ev_x2 = EVV.v

        tap("x2", z1[:, 0:HID], [(EVV, ev_x2)])

        esE_L = _mkstack()
        x2f = esE_L.enter_context(sb("x2f", [128, 16 * T], BF16, side="left"))

        @block.sync
        def _(sync):
            sync.wait_ge(EVV.h, ev_x2)
            for tt in range(NT):
                SPL.inc(sync.dma_start(out=x2_dram[tt * 128:(tt + 1) * 128, :],
                                       in_=z1[:, tt * HID:(tt + 1) * HID]), 16)
        spl_x2 = SPL.v

        mm_x2tr = [0]
        ev_x2evac = [0]
        for tt in range(NT):
            @block.vector
            def _(vector, tt=tt):
                if tt == 0:
                    vector.wait_ge(EVV.h, ev_x2)
                if mm_x2tr[0]:
                    vector.wait_ge(MM.h, mm_x2tr[0])
                EVV.inc(vector.tensor_copy(out=x2bstg[:], in_=z1[:, tt * HID:(tt + 1) * HID]))
            ev_x2c = EVV.v

            @block.tensor
            def _(tensor, tt=tt, ev_x2c=ev_x2c):
                tensor.wait_ge(EVV.h, ev_x2c)
                tensor.wait_ge(EVV.h, ev_x2evac[0] if ev_x2evac[0] else ev_ptb_free[0])
                last = None
                for k in range(16):
                    last = tensor.transpose(out=pTB[:, k * 128:(k + 1) * 128],
                                     in_=x2bstg[:, k * 128:(k + 1) * 128],
                                     identity=id128b[:])
                mm_x2tr[0] = MM.inc(last)
            mmx = MM.v

            @block.vector
            def _(vector, tt=tt, mmx=mmx):
                vector.wait_ge(MM.h, mmx)
                dst = x2f[:].rearrange("q (k t) -> q k t", k=16)[:, :, tt * 128:(tt + 1) * 128]
                ev_x2evac[0] = EVV.inc(vector.tensor_copy(
                    out=dst, in_=pTB[:].rearrange("q (k t) -> q k t", k=16)))

        esD.close()


        if stop_after < 7:
            @block.sync
            def _(sync):
                sync.wait_ge(EVV.h, EVV.v)
                sync.wait_ge(EVS.h, EVS.v)
                sync.wait_ge(EVP.h, EVP.v)
                sync.wait_ge(MM.h, MM.v)
                sync.wait_ge(SPL.h, SPL.v)
                sync.wait_ge(STO.h, STO.v)
            _close_phase_stacks()
            return nc, tap_shapes

        h_bf = esE_L.enter_context(sb("h_bf", [128, 64 * 512], BF16, side="left"))
        esE = _mkstack()
        EE = esE.enter_context
        wrawE = [EE(sb(f"wrawE{i}", [128, 4096], F32, side="right")) for i in range(2)]
        wbfE = [EE(sb(f"wbfE{i}", [128, 4096], BF16, side="right")) for i in range(2)]
        sgu = EE(sb("sgu", [128, 512], F32, side="right"))
        z2h = EE(sb("z2h", [128, 4 * HID], F32, side="right"))
        dnraw = [EE(sb(f"dnraw{i}", [128, 512], F32, side="right")) for i in range(2)]
        dnbf = [EE(sb(f"dnbf{i}", [128, 512], BF16, side="right")) for i in range(2)]
        ztmpE = EE(sb("ztmpE", [128, 512], F32, side="right"))
        lnscr2 = EE(sb("lnscr2", [128, 2048], F32, side="right"))

        ev_h_prev = {}
        mm_dn_last = [0]
        ev_dn_evac = [0]
        ev_dn_tr = [0]
        ev_silu_last = [0]

        @block.sync
        def _(sync, e1=EVV.v, e2=EVS.v, e3=EVP.v, e4=MM.v, e5=SPL.v):
            sync.wait_ge(EVV.h, e1)
            sync.wait_ge(EVS.h, e2)
            sync.wait_ge(EVP.h, e3)
            sync.wait_ge(MM.h, e4)
            sync.wait_ge(SPL.h, e5)

        for hf in range(2):
            @block.sync
            def _(sync, hf=hf):
                if hf == 0:
                    sync.wait_ge(SPL.h, spl_x2)
                else:
                    sync.wait_ge(STO.h, STO.v)
                for q in range(4):
                    tt = hf * 4 + q
                    LD.inc(sync.dma_start(out=z2h[:, q * HID:(q + 1) * HID],
                                          in_=x2_dram[tt * 128:(tt + 1) * 128, :]), 16)
            ld_z2 = LD.v

            cvtE = [0] * 64
            mmE1 = [0] * 64
            for j in range(64):
                @block.sync
                def _(sync, hf=hf, j=j, snap=CVT.v):
                    sync.wait_ge(CVT.h, cvtE[j - 2] if j >= 2 else snap)
                    WLDP[j % 2].inc(sync.dma_start(out=wrawE[j % 2][:, 0:2048], in_=gu_hbm[j]), 16)
                    WLDP[j % 2].inc(sync.dma_start(out=wrawE[j % 2][:, 2048:4096], in_=gu_hbm[64 + j]), 16)
                wldE = WLDP[j % 2].v

                @block.gpsimd
                def _(gpsimd, hf=hf, j=j, wldE=wldE):
                    gpsimd.wait_ge(WLDP[j % 2].h, wldE)
                    if j >= 2:
                        gpsimd.wait_ge(MM.h, mmE1[j - 2])
                    cvtE[j] = CVT.inc(gpsimd.tensor_copy(out=wbfE[j % 2][:], in_=wrawE[j % 2][:]))

                @block.tensor
                def _(tensor, hf=hf, j=j):
                    tensor.wait_ge(CVT.h, cvtE[j])
                    if j >= 2:
                        tensor.wait_ge(EVV.h, ev_h_prev[(hf, j - 2)])
                    elif hf == 1 or ev_dn_evac[0]:
                        tensor.wait_ge(EVV.h, ev_dn_evac[0])
                    last = None
                    for part in range(2):
                        bk = (j % 2) * 2 + part
                        for k in range(16):
                            last = tensor.matmul(
                                out=pF[:, bk * 512:(bk + 1) * 512],
                                lhsT=wbfE[j % 2][:, part * 2048 + k * 128:part * 2048 + (k + 1) * 128],
                                rhs=x2f[:, k * T + hf * 512:k * T + hf * 512 + 512],
                                start=(k == 0), stop=(k == 15))
                    mmE1[j] = MM.inc(last)

                @block.scalar
                def _(scalar, hf=hf, j=j):
                    scalar.wait_ge(MM.h, mmE1[j])
                    EVS.inc(scalar.activation(out=sgu[:], in_=pF[:, ((j % 2) * 2) * 512:
                                                                ((j % 2) * 2 + 1) * 512],
                                              func=AF.Silu))
                ev_siluE = EVS.v

                ev_silu_last[0] = ev_siluE

                @block.vector
                def _(vector, hf=hf, j=j, ev_siluE=ev_siluE):
                    vector.wait_ge(EVS.h, ev_siluE)
                    if hf == 1 and j == 0:
                        vector.wait_ge(MM.h, mm_dn_last[0])
                    bku = (j % 2) * 2 + 1
                    ev_h_prev[(hf, j)] = EVV.inc(vector.tensor_tensor(
                        out=h_bf[:, j * 512:(j + 1) * 512], in0=sgu[:],
                        in1=pF[:, bku * 512:(bku + 1) * 512], op=ALU.mult))
            for dg in range(4):
                cvtDn = [0] * 64
                mmDn = [0] * 64
                for j in range(64):
                    @block.sync
                    def _(sync, hf=hf, dg=dg, j=j, snap=CVT.v):
                        sync.wait_ge(CVT.h, cvtDn[j - 2] if j >= 2 else snap)
                        DNP[j % 2].inc(sync.dma_start(out=dnraw[j % 2][:], in_=dn_hbm[dg, j]), 16)
                    wldDn = DNP[j % 2].v

                    @block.gpsimd
                    def _(gpsimd, hf=hf, dg=dg, j=j, wldDn=wldDn):
                        gpsimd.wait_ge(DNP[j % 2].h, wldDn)
                        if j >= 2:
                            gpsimd.wait_ge(MM.h, mmDn[j - 2])
                        cvtDn[j] = CVT.inc(gpsimd.tensor_copy(out=dnbf[j % 2][:],
                                                              in_=dnraw[j % 2][:]))

                    @block.tensor
                    def _(tensor, hf=hf, dg=dg, j=j):
                        tensor.wait_ge(CVT.h, cvtDn[j])
                        if j == 0:
                            tensor.wait_ge(EVV.h, ev_h_prev[(hf, 63)])
                            tensor.wait_ge(EVS.h, ev_silu_last[0])
                            if ev_dn_evac[0]:
                                tensor.wait_ge(EVV.h, ev_dn_evac[0])
                        last = None
                        for mq in range(4):
                            last = tensor.matmul(out=pF[:, mq * 512:(mq + 1) * 512],
                                          lhsT=dnbf[j % 2][:, mq * 128:(mq + 1) * 128],
                                          rhs=h_bf[:, j * 512:(j + 1) * 512],
                                          start=(j == 0), stop=(j == 63))
                        mmDn[j] = MM.inc(last)

                mm_dg = MM.v
                mm_dn_last[0] = mm_dg
                for mq in range(4):
                    m = dg * 4 + mq

                    @block.vector
                    def _(vector, hf=hf, m=m, mq=mq, mm_dg=mm_dg):
                        vector.wait_ge(MM.h, mm_dg)
                        if ev_dn_tr[0]:
                            vector.wait_ge(MM.h, ev_dn_tr[0])
                        EVV.inc(vector.tensor_copy(out=ztmpE[:], in_=pF[:, mq * 512:(mq + 1) * 512]))
                    ev_ze = EVV.v

                    @block.tensor
                    def _(tensor, hf=hf, m=m, ev_ze=ev_ze):
                        tensor.wait_ge(EVV.h, ev_ze)
                        if ev_dn_evac[0]:
                            tensor.wait_ge(EVV.h, ev_dn_evac[0])
                        last = None
                        for q in range(4):
                            last = tensor.transpose(out=pF[:, 2048 + q * 128:2048 + (q + 1) * 128],
                                             in_=ztmpE[:, q * 128:(q + 1) * 128],
                                             identity=id128f[:])
                        MM.inc(last)
                        ev_dn_tr[0] = MM.v
                    mm_tr = MM.v

                    @block.vector
                    def _(vector, hf=hf, m=m, mm_tr=mm_tr, ld_z2=ld_z2):
                        vector.wait_ge(MM.h, mm_tr)
                        vector.wait_ge(LD.h, ld_z2)
                        last = None
                        for q in range(4):
                            last = vector.tensor_add(
                                out=z2h[:, q * HID + m * 128:q * HID + (m + 1) * 128],
                                in0=pF[:, 2048 + q * 128:2048 + (q + 1) * 128],
                                in1=z2h[:, q * HID + m * 128:q * HID + (m + 1) * 128])
                        ev_dn_evac[0] = EVV.inc(last)

            @block.vector
            def _(vector, hf=hf):
                vector.wait_ge(EVV.h, EVV.v)
                for q in range(4):
                    zt = z2h[:, q * HID:(q + 1) * HID]
                    vector.scalar_tensor_tensor(out=lnscr2[:], in0=zt, scalar=1.0, in1=zt,
                                                op0=ALU.mult, op1=ALU.mult,
                                                accum_out=ssq16[:, 8 + q:9 + q])
                    vector.tensor_scalar(out=lnscr2[:], in0=zt, scalar1=0.0, scalar2=0.0,
                                         op0=ALU.add, op1=ALU.add,
                                         accum_out=sums16[:, 8 + q:9 + q])
                EVV.inc(vector.engine_nop())
            ev_ln2acc = EVV.v

            @block.gpsimd
            def _(gpsimd, hf=hf, ev_ln2acc=ev_ln2acc):
                gpsimd.wait_ge(EVV.h, ev_ln2acc)
                gpsimd.tensor_scalar_mul(out=mucol[:, 8:12], in0=sums16[:, 8:12], scalar1=1.0 / HID)
                gpsimd.tensor_scalar_mul(out=rstd[:, 8:12], in0=ssq16[:, 8:12], scalar1=1.0 / HID)
                gpsimd.tensor_mul(out=sums16[:, 8:12], in0=mucol[:, 8:12], in1=mucol[:, 8:12])
                gpsimd.tensor_sub(out=rstd[:, 8:12], in0=rstd[:, 8:12], in1=sums16[:, 8:12])
                EVP.inc(gpsimd.tensor_scalar_add(out=rstd[:, 8:12], in0=rstd[:, 8:12],
                                                 scalar1=EPS))
            ev_ln2v = EVP.v

            @block.scalar
            def _(scalar, hf=hf, ev_ln2v=ev_ln2v):
                scalar.wait_ge(EVP.h, ev_ln2v)
                scalar.activation(out=rstd[:, 8:12], in_=rstd[:, 8:12], func=AF.Ln)
                EVS.inc(scalar.activation(out=rstd[:, 8:12], in_=rstd[:, 8:12], func=AF.Exp,
                                          scale=-0.5))
            ev_ln2s = EVS.v

            @block.vector
            def _(vector, hf=hf, ev_ln2s=ev_ln2s):
                vector.wait_ge(EVS.h, ev_ln2s)
                last = None
                for q in range(4):
                    zt = z2h[:, q * HID:(q + 1) * HID]
                    last = vector.tensor_scalar(out=zt, in0=zt, scalar1=mucol[:, 8 + q:9 + q],
                                                scalar2=rstd[:, 8 + q:9 + q],
                                                op0=ALU.subtract, op1=ALU.mult)
                EVV.inc(last)
            ev_out = EVV.v

            @block.sync
            def _(sync, hf=hf, ev_out=ev_out):
                sync.wait_ge(EVV.h, ev_out)
                for q in range(4):
                    tt = hf * 4 + q
                    STO.inc(sync.dma_start(out=out_hbm[tt * 128:(tt + 1) * 128, :],
                                           in_=z2h[:, q * HID:(q + 1) * HID]), 16)

        esE.close()
        esE_L.close()

        @block.sync
        def _(sync):
            sync.wait_ge(STO.h, STO.v)

    return nc, tap_shapes


def _strips(wT, nk, nm):
    a = wT.reshape(nk, 128, nm, 128).transpose(2, 1, 0, 3).reshape(nm, 128, nk * 128)
    return np.ascontiguousarray(a)


def prepare_in_maps(L0_attn_x, L0_attn_W_in, L0_attn_conv_w, L0_attn_conv_b,
                    L0_attn_dt_bias, L0_attn_A_log, L0_attn_D, L0_attn_norm_w,
                    L0_attn_W_out, L0_ffn_w_gate_up, L0_ffn_w_down):
    x2d = np.ascontiguousarray(np.asarray(L0_attn_x).reshape(8192, HID))

    winT = np.zeros((HID, MC_ALL * 128), dtype=np.float32)
    winT[:, :PROJ] = np.asarray(L0_attn_W_in).T
    w_in_s = _strips(winT, 16, MC_ALL)

    w_out_s = _strips(np.ascontiguousarray(np.asarray(L0_attn_W_out).T), 32, 16)
    w_gu_s = _strips(np.ascontiguousarray(np.asarray(L0_ffn_w_gate_up).T), 16, 128)

    dnT = np.ascontiguousarray(np.asarray(L0_ffn_w_down).T)
    w_dn_s = np.ascontiguousarray(
        dnT.reshape(64, 128, 4, 4, 128).transpose(2, 0, 1, 3, 4).reshape(4, 64, 128, 512))

    conv_w = np.ascontiguousarray(
        np.asarray(L0_attn_conv_w).reshape(MC_HBC, 128, 4).transpose(1, 0, 2).reshape(128, MC_HBC * 4))
    conv_b = np.ascontiguousarray(np.asarray(L0_attn_conv_b).reshape(MC_HBC, 128).T)
    dt_bias = np.ascontiguousarray(np.asarray(L0_attn_dt_bias).reshape(64, 1))
    a_log = np.ascontiguousarray(np.asarray(L0_attn_A_log).reshape(64, 1))
    d_nn = np.ascontiguousarray(np.broadcast_to(np.asarray(L0_attn_D), (128, 64)))
    normw_g = np.ascontiguousarray(np.asarray(L0_attn_norm_w).reshape(32, 128).T)

    shared = dict(w_in_s=w_in_s, w_out_s=w_out_s, w_gu_s=w_gu_s, w_dn_s=w_dn_s,
                  conv_w=conv_w, conv_b=conv_b, dt_bias=dt_bias, a_log=a_log,
                  d_n=d_nn, normw_g=normw_g)

    in_maps = []
    for c in range(NCORES):
        e = (np.arange(8) < c).astype(np.float32)
        em = np.concatenate([e, 1.0 - e])
        emask = np.ascontiguousarray(np.broadcast_to(em, (128, 16)).astype(np.float32))
        xprev = np.zeros((3, HID), np.float32) if c == 0 else \
            np.ascontiguousarray(x2d[c * T - 3:c * T])
        m = dict(shared)
        m.update(x_tok=np.ascontiguousarray(x2d[c * T:(c + 1) * T]),
                 xprev=xprev, emask=emask)
        in_maps.append(m)
    return in_maps


def kernel(**inputs):
    in_maps = prepare_in_maps(**inputs)
    nc, _ = build_program()
    res = run_bass_kernel_spmd(nc, in_maps, core_ids=list(range(NCORES)))
    out = np.concatenate([r["out"] for r in res.results], axis=0)
    return out.reshape(1, 8192, HID)


# version 03
# speedup vs baseline: 1.0046x; 1.0046x over previous
import contextlib
import numpy as np
import ml_dtypes
import concourse.bass as bass
import concourse.mybir as mybir
from concourse.bass_utils import run_bass_kernel_spmd

F32 = mybir.dt.float32
BF16 = mybir.dt.bfloat16
AF = mybir.ActivationFunctionType
ALU = mybir.AluOpType
BF = ml_dtypes.bfloat16

NCORES = 8
T = 1024
NT = 8
HID = 2048
I = 4096
CONV = 4352
PROJ = 8512
INTER = 8192
EPS = 1e-5

MC_GATE = 32
MC_HBC = 34
MC_ALL = 67
HBW = 1030


class Sem:
    def __init__(self, stack, nc, name):
        self.h = stack.enter_context(nc.semaphore(name))
        self.v = 0

    def inc(self, inst, n=1):
        inst.then_inc(self.h, n)
        self.v += n
        return self.v


def build_program(taps=(), stop_after=99):
    taps = set(taps)
    nc = bass.Bass("TRN2", target_bir_lowering=False)

    x_hbm = nc.dram_tensor("x_tok", [T, HID], F32, kind="ExternalInput").ap()
    xprev_hbm = nc.dram_tensor("xprev", [3, HID], F32, kind="ExternalInput").ap()
    win_hbm = nc.dram_tensor("w_in_s", [MC_ALL, 128, 2048], F32, kind="ExternalInput").ap()
    wout_hbm = nc.dram_tensor("w_out_s", [16, 128, 4096], F32, kind="ExternalInput").ap()
    gu_hbm = nc.dram_tensor("w_gu_s", [128, 128, 2048], F32, kind="ExternalInput").ap()
    dn_hbm = nc.dram_tensor("w_dn_s", [4, 64, 128, 512], F32, kind="ExternalInput").ap()
    convw_hbm = nc.dram_tensor("conv_w", [128, 136], F32, kind="ExternalInput").ap()
    convb_hbm = nc.dram_tensor("conv_b", [128, 34], F32, kind="ExternalInput").ap()
    dtb_hbm = nc.dram_tensor("dt_bias", [64, 1], F32, kind="ExternalInput").ap()
    alog_hbm = nc.dram_tensor("a_log", [64, 1], F32, kind="ExternalInput").ap()
    dN_hbm = nc.dram_tensor("d_n", [128, 64], F32, kind="ExternalInput").ap()
    normw_hbm = nc.dram_tensor("normw_g", [128, 32], F32, kind="ExternalInput").ap()
    emask_hbm = nc.dram_tensor("emask", [128, 16], F32, kind="ExternalInput").ap()

    out_hbm = nc.dram_tensor("out", [T, HID], F32, kind="ExternalOutput").ap()

    sg_dram = nc.dram_tensor("sg_scr", [MC_GATE, 128, T], BF16, kind="Internal").ap()
    x2_dram = nc.dram_tensor("x2_scr", [T, HID], F32, kind="Internal").ap()

    e128 = np.eye(128)
    id128b_h = nc.inline_tensor(e128.astype(BF), "c_id128b")
    id64b_h = nc.inline_tensor(np.eye(64).astype(BF), "c_id64b")
    id128f_h = nc.inline_tensor(e128.astype(np.float32), "c_id128f")
    id64f_h = nc.inline_tensor(np.eye(64, dtype=np.float32), "c_id64f")
    trigt_h = nc.inline_tensor(np.tril(np.ones((128, 128)), -1).astype(BF), "c_trigt")
    trile_h = nc.inline_tensor(np.triu(np.ones((128, 128)), 0).astype(BF), "c_trile")
    trige_h = nc.inline_tensor(np.triu(np.ones((128, 128)), 0).astype(np.float32), "c_trige")

    tap_shapes = {}
    phase_stacks = []

    def _mkstack():
        st = contextlib.ExitStack()
        phase_stacks.append(st)
        return st

    def _close_phase_stacks():
        for st in reversed(phase_stacks):
            st.close()

    es = contextlib.ExitStack()
    with es:
        E = es.enter_context

        def sb(name, shape, dtype, side="left"):
            return nc.sbuf_tensor(name, shape, dtype, side=side)

        id128b = E(sb("id128b", [128, 128], BF16))
        id64b = E(sb("id64b", [64, 64], BF16))
        id128f = E(sb("id128f", [128, 128], F32))
        id64f = E(sb("id64f", [64, 64], F32))
        trigt = E(sb("trigt", [128, 128], BF16))
        trile = E(sb("trile", [128, 128], BF16))
        trige = E(sb("trige", [128, 128], F32))
        convw = E(sb("sb_convw", [128, 136], F32))
        convb = E(sb("sb_convb", [128, 34], F32))
        dtb = E(sb("dtb", [64, 1], F32))
        alog = E(sb("alog", [64, 1], F32))
        negA = E(sb("negA", [64, 1], F32))
        d_n = E(sb("sb_d_n", [128, 64], F32))
        normw = E(sb("sb_normw", [128, 32], F32))
        emask = E(sb("sb_emask", [128, 16], F32))
        ones64 = E(sb("ones64", [64, 1], F32))
        dA_b = E(sb("dA_b", [64, T], BF16))
        lndt_b = E(sb("lndt_b", [64, T], BF16))
        acumT = E(sb("acumT", [64, T], F32))
        alsum = E(sb("alsum", [64, 1], F32))
        rinv8 = E(sb("rinv8", [128, 8], F32))
        dcrN = E(sb("dcrN", [128, 64], F32))
        sums16 = E(sb("sums16", [128, 16], F32))
        ssq16 = E(sb("ssq16", [128, 16], F32))
        mucol = E(sb("mucol", [128, 16], F32))
        rstd = E(sb("rstd", [128, 16], F32))

        pF = E(nc.psum_tensor("pF", [128, 2560], F32))
        pX = E(nc.psum_tensor("pX", [128, 512], F32))
        pTB = E(nc.psum_tensor("pTB", [128, 2048], BF16))

        LD = Sem(es, nc, "LD")
        LDX = [Sem(es, nc, "LDX0"), Sem(es, nc, "LDX1")]
        WLD = Sem(es, nc, "WLD")
        WLDP = [Sem(es, nc, "WLDP0"), Sem(es, nc, "WLDP1")]
        DNP = [Sem(es, nc, "DNP0"), Sem(es, nc, "DNP1")]
        CVT = Sem(es, nc, "CVT")
        MM = Sem(es, nc, "MM")
        EVV = Sem(es, nc, "EVV")
        EVS = Sem(es, nc, "EVS")
        EVP = Sem(es, nc, "EVP")
        SPL = Sem(es, nc, "SPL")
        SPLP = [Sem(es, nc, "SPLP0"), Sem(es, nc, "SPLP1")]
        CC = Sem(es, nc, "CC")
        GDM = Sem(es, nc, "GDM")
        STO = Sem(es, nc, "STO")

        block = E(nc.Block())

        def tap(name, ap_src, wait_sems):
            if name not in taps:
                return
            th = nc.dram_tensor("tap_" + name, list(ap_src.shape), ap_src.dtype,
                                kind="ExternalOutput")
            tap_shapes[name] = tuple(ap_src.shape)

            @block.sync
            def _(sync):
                for s, v in wait_sems:
                    sync.wait_ge(s.h, v)
                STO.inc(sync.dma_start(out=th.ap(), in_=ap_src), 16)

        esDT = _mkstack()
        EDT = esDT.enter_context
        dt_raw = EDT(sb("dt_raw", [64, T], F32))
        dt_f = EDT(sb("dt_f", [64, T], F32))
        dA_f = EDT(sb("dA_f", [64, T], F32))
        tsc64 = EDT(sb("tsc64", [64, T], F32))

        esHBC = _mkstack()
        hbc = esHBC.enter_context(sb("hbc", [128, MC_HBC * HBW], BF16))

        esAB = _mkstack()
        EA = esAB.enter_context
        x_bf = EA(sb("x_bf", [128, 16 * T], BF16, side="right"))
        xstg = [EA(sb(f"xstg{i}", [128, HID], F32, side="right")) for i in range(2)]
        xstgb = EA(sb("xstgb", [128, HID], BF16, side="right"))
        xprevstg = EA(sb("xprevstg", [3, HID], F32, side="right"))
        xprevstgb = EA(sb("xprevstgb", [3, HID], BF16, side="right"))
        xprevf = EA(sb("xprevf", [128, 64], BF16, side="right"))
        wraw = [EA(sb(f"wraw{i}", [128, 2048], F32, side="right")) for i in range(2)]
        wbf = [EA(sb(f"wbf{i}", [128, 2048], BF16, side="right")) for i in range(2)]
        sgf = EA(sb("sgf", [128, 512], BF16, side="right"))
        sgf2 = EA(sb("sgf2", [128, 512], BF16, side="right"))

        @block.sync
        def _(sync):
            for dst, src in (
                (id128b, id128b_h.ap()), (id64b, id64b_h.ap()), (id128f, id128f_h.ap()),
                (id64f, id64f_h.ap()),
                (trigt, trigt_h.ap()), (trile, trile_h.ap()), (trige, trige_h.ap()),
                (convw, convw_hbm), (convb, convb_hbm), (dtb, dtb_hbm),
                (alog, alog_hbm), (d_n, dN_hbm), (normw, normw_hbm),
                (emask, emask_hbm), (xprevstg, xprev_hbm),
            ):
                LD.inc(sync.dma_start(out=dst[:], in_=src), 16)
        ld_const = LD.v

        @block.vector
        def _(vector):
            vector.wait_ge(LD.h, ld_const)
            vector.memset(ones64[:], 1.0)
            EVV.inc(vector.tensor_copy(out=xprevstgb[:], in_=xprevstg[:]))
        ev_xprevc = EVV.v

        @block.tensor
        def _(tensor):
            tensor.wait_ge(EVV.h, ev_xprevc)
            last = None
            for kk in range(16):
                last = tensor.transpose(out=pTB[:, kk * 4:kk * 4 + 3],
                                 in_=xprevstgb[0:3, kk * 128:(kk + 1) * 128],
                                 identity=id128b[0:3, 0:3])
            MM.inc(last)
        mm_xprev = MM.v

        @block.vector
        def _(vector):
            vector.wait_ge(MM.h, mm_xprev)
            EVV.inc(vector.tensor_copy(
                out=xprevf[:].rearrange("q (k w) -> q k w", k=16)[:, :, 0:3],
                in_=pTB[:, 0:64].rearrange("q (k w) -> q k w", k=16)[:, :, 0:3]))
        ev_xprevf = EVV.v

        ld_x = [0] * NT
        ev_xcast = [0] * NT
        mm_xtr = [0] * NT
        ev_xevac = [0] * NT
        for i in range(NT):
            @block.sync
            def _(sync, i=i):
                if i >= 2:
                    sync.wait_ge(EVV.h, ev_xcast[i - 2])
                ld_x[i] = LDX[i % 2].inc(sync.dma_start(
                    out=xstg[i % 2][:], in_=x_hbm[i * 128:(i + 1) * 128, :]), 16)

            @block.vector
            def _(vector, i=i):
                vector.wait_ge(LDX[i % 2].h, ld_x[i])
                if i >= 1:
                    vector.wait_ge(MM.h, mm_xtr[i - 1])
                ev_xcast[i] = EVV.inc(vector.tensor_copy(out=xstgb[:], in_=xstg[i % 2][:]))

            @block.tensor
            def _(tensor, i=i):
                tensor.wait_ge(EVV.h, ev_xcast[i])
                tensor.wait_ge(EVV.h, ev_xevac[i - 1] if i >= 1 else ev_xprevf)
                last = None
                for k in range(16):
                    last = tensor.transpose(out=pTB[:, k * 128:(k + 1) * 128],
                                     in_=xstgb[:, k * 128:(k + 1) * 128],
                                     identity=id128b[:])
                mm_xtr[i] = MM.inc(last)

            @block.vector
            def _(vector, i=i):
                vector.wait_ge(MM.h, mm_xtr[i])
                dst = x_bf[:].rearrange("q (k t) -> q k t", k=16)[:, :, i * 128:(i + 1) * 128]
                ev_xevac[i] = EVV.inc(vector.tensor_copy(
                    out=dst, in_=pTB[:].rearrange("q (k t) -> q k t", k=16)))


        if stop_after < 1:
            _close_phase_stacks()
            return nc, tap_shapes

        wld = [0] * MC_ALL
        cvt_done = [0] * MC_ALL
        mm_strip = [0] * MC_ALL
        mme = [[0, 0] for _ in range(MC_ALL)]
        evd = {}
        spl_prev = [0]
        spl_par = [0, 0]
        evp_prev = [0]
        evs_prev = [0]

        for m in range(MC_ALL):
            @block.sync
            def _(sync, m=m):
                if m >= 2:
                    sync.wait_ge(CVT.h, cvt_done[m - 2])
                wld[m] = WLDP[m % 2].inc(sync.dma_start(out=wraw[m % 2][:], in_=win_hbm[m]), 16)

            @block.gpsimd
            def _(gpsimd, m=m):
                gpsimd.wait_ge(WLDP[m % 2].h, wld[m])
                if m >= 2:
                    gpsimd.wait_ge(MM.h, mm_strip[m - 2])
                cvt_done[m] = CVT.inc(gpsimd.tensor_copy(out=wbf[m % 2][:], in_=wraw[m % 2][:]))

            @block.tensor
            def _(tensor, m=m):
                tensor.wait_ge(CVT.h, cvt_done[m])
                if m == 0:
                    tensor.wait_ge(EVV.h, ev_xevac[NT - 1])
                for tb in range(2):
                    bk = (m % 2) * 2 + tb
                    if m >= 2 and (m - 2, tb) in evd:
                        s, v = evd[(m - 2, tb)]
                        tensor.wait_ge(s.h, v)
                    last = None
                    for k in range(16):
                        last = tensor.matmul(out=pF[:, bk * 512:(bk + 1) * 512],
                                      lhsT=wbf[m % 2][:, k * 128:(k + 1) * 128],
                                      rhs=x_bf[:, k * T + tb * 512:k * T + tb * 512 + 512],
                                      start=(k == 0), stop=(k == 15))
                        if tb == 0 and 32 <= m <= 65:
                            last = tensor.matmul(out=pX[:, (m - 32) * 4:(m - 32) * 4 + 3],
                                          lhsT=wbf[m % 2][:, k * 128:(k + 1) * 128],
                                          rhs=xprevf[:, k * 4:k * 4 + 3],
                                          start=(k == 0), stop=(k == 15))
                    mme[m][tb] = MM.inc(last)
                mm_strip[m] = MM.v

            for tb in range(2):
                bk = (m % 2) * 2 + tb
                if m < MC_GATE:
                    @block.scalar
                    def _(scalar, m=m, tb=tb, bk=bk):
                        par = (m * 2 + tb) % 2
                        buf = sgf if par == 0 else sgf2
                        scalar.wait_ge(MM.h, mme[m][tb])
                        if spl_par[par]:
                            scalar.wait_ge(SPLP[par].h, spl_par[par])
                        evs_prev[0] = EVS.inc(scalar.activation(
                            out=buf[:], in_=pF[:, bk * 512:(bk + 1) * 512], func=AF.Silu))
                        scalar.wait_ge(EVS.h, evs_prev[0])
                        spl_par[par] = SPLP[par].inc(scalar.dma_start(
                            out=sg_dram[m][:, tb * 512:tb * 512 + 512], in_=buf[:]), 16)
                    evd[(m, tb)] = (EVS, evs_prev[0])
                elif m < MC_GATE + MC_HBC:
                    @block.vector
                    def _(vector, m=m, tb=tb, bk=bk):
                        vector.wait_ge(MM.h, mme[m][tb])
                        cb = (m - 32) * HBW
                        v = EVV.inc(vector.tensor_copy(
                            out=hbc[:, cb + 3 + tb * 512:cb + 3 + tb * 512 + 512],
                            in_=pF[:, bk * 512:(bk + 1) * 512]))
                        evd[(m, tb)] = (EVV, v)
                else:
                    @block.vector
                    def _(vector, m=m, tb=tb, bk=bk):
                        vector.wait_ge(MM.h, mme[m][tb])
                        v = EVV.inc(vector.tensor_copy(
                            out=dt_raw[:, tb * 512:tb * 512 + 512],
                            in_=pF[0:64, bk * 512:(bk + 1) * 512]))
                        evd[(m, tb)] = (EVV, v)


        if stop_after < 2:

            @block.sync
            def _(sync):
                sync.wait_ge(EVV.h, EVV.v)
                sync.wait_ge(EVS.h, EVS.v)
                sync.wait_ge(EVP.h, EVP.v)
                sync.wait_ge(MM.h, MM.v)
                sync.wait_ge(SPL.h, SPL.v)
                sync.wait_ge(STO.h, STO.v)
            _close_phase_stacks()
            return nc, tap_shapes

        @block.vector
        def _(vector):
            vector.wait_ge(MM.h, mme[65][0])
            EVV.inc(vector.tensor_copy(
                out=hbc[:].rearrange("q (m w) -> q m w", m=MC_HBC)[:, :, 0:3],
                in_=pX[:, 0:136].rearrange("q (m w) -> q m w", m=MC_HBC)[:, :, 0:3]))
        ev_hbc_all = EVV.v

        tap("dt_raw", dt_raw[:], [(EVV, ev_hbc_all)])
        tap("hbc", hbc[:, 0:4 * HBW], [(EVV, ev_hbc_all)])

        @block.vector
        def _(vector):
            EVV.inc(vector.tensor_scalar_add(out=tsc64[:], in0=dt_raw[:], scalar1=dtb[:, 0:1]))
        ev_z = EVV.v

        @block.scalar
        def _(scalar):
            scalar.wait_ge(EVV.h, ev_z)
            scalar.activation(out=dt_f[:], in_=tsc64[:], func=AF.Abs)
            scalar.activation(out=dt_f[:], in_=dt_f[:], func=AF.Exp, scale=-1.0)
            EVS.inc(scalar.activation(out=dt_f[:], in_=dt_f[:], func=AF.Ln, bias=ones64[:, 0:1]))
            EVS.inc(scalar.activation(out=negA[:], in_=alog[:], func=AF.Exp))
        ev_ln1p = EVS.v

        @block.vector
        def _(vector):
            vector.wait_ge(EVS.h, ev_ln1p)
            vector.tensor_scalar_max(out=dt_raw[:], in0=tsc64[:], scalar1=0.0)
            vector.tensor_add(out=dt_f[:], in0=dt_f[:], in1=dt_raw[:])
            vector.tensor_scalar(out=dA_f[:], in0=dt_f[:], scalar1=negA[:, 0:1],
                                 scalar2=-1.0, op0=ALU.mult, op1=ALU.mult)
            EVV.inc(vector.tensor_copy(out=dA_b[:], in_=dA_f[:]))
        ev_dt = EVV.v

        @block.scalar
        def _(scalar):
            scalar.wait_ge(EVV.h, ev_dt)
            EVS.inc(scalar.activation(out=tsc64[:], in_=dt_f[:], func=AF.Ln))
        ev_lndt = EVS.v

        @block.vector
        def _(vector):
            vector.wait_ge(EVS.h, ev_lndt)
            EVV.inc(vector.tensor_copy(out=lndt_b[:], in_=tsc64[:]))

        tap("dt_f", dt_f[:], [(EVV, ev_dt)])
        tap("dA_f", dA_f[:], [(EVV, ev_dt)])
        tap("negA", negA[:], [(EVV, ev_dt)])

        esAB.close()
        esB = _mkstack()
        EB = esB.enter_context
        xs_tok = EB(sb("xs_tok", [128, NT * I], BF16, side="right"))
        b_n = EB(sb("b_n", [128, T], BF16, side="right"))
        c_n = EB(sb("c_n", [128, T], BF16, side="right"))
        b_tok = EB(sb("b_tok", [128, T], BF16, side="right"))
        dte_tok = EB(sb("dte_tok", [128, NT * 64], F32, side="right"))
        cdecN = EB(sb("cdecN", [128, NT * 64], F32, side="right"))
        carry = EB(sb("carry", [128, I], F32, side="right"))
        shb4 = EB(sb("shb4", [128, 2048], BF16, side="right"))
        accv = EB(sb("accv", [128, 512], F32, side="right"))
        accp = EB(sb("accp", [128, 512], F32, side="right"))
        accp2 = EB(sb("accp2", [128, 512], F32, side="right"))
        xsstg = EB(sb("xsstg", [128, 1024], BF16, side="right"))

        conv_done = [[0, 0] for _ in range(MC_HBC)]
        mm_xs_tr = {}
        ev_xs_evac = [0]

        for m in range(MC_HBC):
            eng_is_v = True
            for tb in range(2):
                if eng_is_v:
                    @block.vector
                    def _(vector, m=m, tb=tb):
                        vector.wait_ge(EVV.h, ev_hbc_all)
                        cb = m * HBW
                        sl = lambda k: hbc[:, cb + tb * 512 + k: cb + tb * 512 + k + 512]
                        vector.tensor_scalar_mul(out=accv[:], in0=sl(0),
                                                 scalar1=convw[:, 4 * m:4 * m + 1])
                        for k in range(1, 3):
                            vector.scalar_tensor_tensor(
                                out=accv[:], in0=sl(k), scalar=convw[:, 4 * m + k:4 * m + k + 1],
                                in1=accv[:], op0=ALU.mult, op1=ALU.add)
                        conv_done[m][tb] = EVV.inc(vector.scalar_tensor_tensor(
                            out=accv[:], in0=sl(3), scalar=convw[:, 4 * m + 3:4 * m + 4],
                            in1=accv[:], op0=ALU.mult, op1=ALU.add))
                else:
                    @block.gpsimd
                    def _(gpsimd, m=m, tb=tb):
                        gpsimd.wait_ge(EVV.h, ev_hbc_all)
                        cb = m * HBW
                        sl = lambda k: hbc[:, cb + tb * 512 + k: cb + tb * 512 + k + 512]
                        gpsimd.tensor_scalar_mul(out=accp[:], in0=sl(0),
                                                 scalar1=convw[:, 4 * m:4 * m + 1])
                        last = None
                        for k in range(1, 4):
                            gpsimd.tensor_scalar_mul(out=accp2[:], in0=sl(k),
                                                     scalar1=convw[:, 4 * m + k:4 * m + k + 1])
                            last = gpsimd.tensor_add(out=accp[:], in0=accp[:], in1=accp2[:])
                        conv_done[m][tb] = EVP.inc(last)

                @block.scalar
                def _(scalar, m=m, tb=tb, eng_is_v=eng_is_v):
                    sem = EVV if eng_is_v else EVP
                    scalar.wait_ge(sem.h, conv_done[m][tb])
                    prev_tr = [mm_xs_tr[mm] for mm in mm_xs_tr if mm < m]
                    if m < 32 and prev_tr:
                        scalar.wait_ge(MM.h, prev_tr[-1])
                    acc = accv if eng_is_v else accp
                    if m < 32:
                        dst = xsstg[:, tb * 512:tb * 512 + 512]
                    elif m == 32:
                        dst = b_n[:, tb * 512:tb * 512 + 512]
                    else:
                        dst = c_n[:, tb * 512:tb * 512 + 512]
                    EVS.inc(scalar.activation(out=dst, in_=acc[:], func=AF.Silu,
                                              bias=convb[:, m:m + 1]))
                s_after = EVS.v
                if eng_is_v:
                    @block.vector
                    def _(vector, v=s_after):
                        vector.wait_ge(EVS.h, v)
                else:
                    @block.gpsimd
                    def _(gpsimd, v=s_after):
                        gpsimd.wait_ge(EVS.h, v)

            if m <= 32:
                src = xsstg if m < 32 else b_n

                @block.tensor
                def _(tensor, m=m, src=src):
                    tensor.wait_ge(EVS.h, EVS.v)
                    tensor.wait_ge(EVV.h, ev_xs_evac[0] if ev_xs_evac[0] else ev_xevac[NT - 1])
                    last = None
                    for c in range(8):
                        last = tensor.transpose(out=pTB[:, c * 128:(c + 1) * 128],
                                         in_=src[:, c * 128:(c + 1) * 128],
                                         identity=id128b[:])
                    mm_xs_tr[m] = MM.inc(last)

                @block.vector
                def _(vector, m=m):
                    vector.wait_ge(MM.h, mm_xs_tr[m])
                    if m < 32:
                        dst = xs_tok[:].rearrange("q (c f) -> q c f", c=NT)[:, :, m * 128:(m + 1) * 128]
                    else:
                        dst = b_tok[:].rearrange("q (c f) -> q c f", c=NT)
                    ev_xs_evac[0] = EVV.inc(vector.tensor_copy(
                        out=dst, in_=pTB[:, 0:1024].rearrange("q (c t) -> q c t", c=8)))

        ev_convB = EVV.v
        tap("xs_tok", xs_tok[:, 0:I], [(EVV, ev_convB)])
        tap("b_n", b_n[:], [(EVV, ev_convB)])
        tap("c_n", c_n[:], [(EVV, ev_convB)])
        tap("b_tok", b_tok[:], [(EVV, ev_convB)])

        esHBC.close()


        if stop_after < 3:
            @block.sync
            def _(sync):
                sync.wait_ge(EVV.h, EVV.v)
                sync.wait_ge(EVS.h, EVS.v)
                sync.wait_ge(EVP.h, EVP.v)
                sync.wait_ge(MM.h, MM.v)
                sync.wait_ge(SPL.h, SPL.v)
                sync.wait_ge(STO.h, STO.v)
            _close_phase_stacks()
            return nc, tap_shapes

        @block.vector
        def _(vector):
            vector.wait_ge(EVV.h, ev_convB)
            vector.memset(carry[:], 0.0)
            EVV.inc(vector.tensor_reduce(out=alsum[:], in_=dA_f[:],
                                         axis=mybir.AxisListType.X, op=ALU.add))

        ev_cdec_prev = [0]
        ev_carry_upd = [0]
        for c in range(NT):
            cs = slice(c * 128, (c + 1) * 128)

            @block.vector
            def _(vector, c=c, cs=cs):
                EVV.inc(vector.tensor_tensor_scan(
                    out=acumT[:, cs], data0=dA_f[:, cs], data1=dA_f[:, cs],
                    initial=0.0, op0=ALU.add, op1=ALU.bypass))
            ev_acum = EVV.v

            @block.scalar
            def _(scalar, c=c, cs=cs):
                scalar.wait_ge(EVV.h, ev_acum)
                EVS.inc(scalar.activation(out=tsc64[:, cs], in_=acumT[:, cs], func=AF.Exp,
                                          scale=-1.0, bias=acumT[:, c * 128 + 127:c * 128 + 128]))
            ev_dte1 = EVS.v

            @block.vector
            def _(vector, c=c, cs=cs):
                vector.wait_ge(EVS.h, ev_dte1)
                EVV.inc(vector.tensor_mul(out=tsc64[:, cs], in0=tsc64[:, cs], in1=dt_f[:, cs]))
            ev_dte2 = EVV.v

            @block.tensor
            def _(tensor, c=c, cs=cs):
                tensor.wait_ge(EVV.h, ev_dte2)
                if c > 0:
                    tensor.wait_ge(EVS.h, ev_cdec_prev[0])
                tensor.transpose(out=pF[:, 2048:2112], in_=tsc64[:, cs], identity=id64f[:])
                MM.inc(tensor.matmul(out=pF[:, 2112:2176],
                              lhsT=acumT[:, c * 128 + 127:c * 128 + 128].broadcast_to([64, 128]),
                              rhs=id64f[:], start=True, stop=True))
            mm_c1 = MM.v

            @block.vector
            def _(vector, c=c):
                vector.wait_ge(MM.h, mm_c1)
                EVV.inc(vector.tensor_copy(out=dte_tok[:, c * 64:(c + 1) * 64],
                                           in_=pF[:, 2048:2112]))
            ev_dtetok = EVV.v

            @block.scalar
            def _(scalar, c=c):
                scalar.wait_ge(MM.h, mm_c1)
                ev_cdec_prev[0] = EVS.inc(scalar.activation(
                    out=cdecN[:, c * 64:(c + 1) * 64], in_=pF[:, 2112:2176], func=AF.Exp))

            for hh in range(2):
                @block.vector
                def _(vector, c=c, hh=hh):
                    vector.wait_ge(EVV.h, ev_dtetok)
                    vector.wait_ge(EVS.h, ev_cdec_prev[0])
                    vector.wait_ge(MM.h, MM.v)
                    EVV.inc(vector.tensor_tensor(
                        out=shb4[:].rearrange("q (h p2) -> q h p2", h=32),
                        in0=xs_tok[:, c * I + hh * 2048:c * I + hh * 2048 + 2048]
                            .rearrange("q (h p2) -> q h p2", h=32),
                        in1=dte_tok[:, c * 64 + hh * 32:c * 64 + hh * 32 + 32].unsqueeze(2)
                            .broadcast_to([128, 32, 64]),
                        op=ALU.mult))
                ev_dxs = EVV.v

                @block.tensor
                def _(tensor, c=c, hh=hh):
                    tensor.wait_ge(EVV.h, ev_dxs)
                    if ev_carry_upd[0]:
                        tensor.wait_ge(EVV.h, ev_carry_upd[0])
                    last = None
                    for j in range(4):
                        last = tensor.matmul(out=pF[:, j * 512:(j + 1) * 512],
                                      lhsT=b_tok[:, c * 128:(c + 1) * 128],
                                      rhs=shb4[:, j * 512:(j + 1) * 512],
                                      start=True, stop=True)
                    MM.inc(last)
                mm_st = MM.v

                @block.vector
                def _(vector, c=c, hh=hh):
                    vector.wait_ge(MM.h, mm_st)
                    ch = carry[:, hh * 2048:hh * 2048 + 2048]
                    vector.tensor_tensor(
                        out=ch.rearrange("q (h p2) -> q h p2", h=32),
                        in0=ch.rearrange("q (h p2) -> q h p2", h=32),
                        in1=cdecN[:, c * 64 + hh * 32:c * 64 + hh * 32 + 32].unsqueeze(2)
                            .broadcast_to([128, 32, 64]),
                        op=ALU.mult)
                    ev_carry_upd[0] = EVV.inc(vector.tensor_add(out=ch, in0=ch, in1=pF[:, 0:2048]))

        tap("acumT", acumT[:], [(EVV, EVV.v)])
        tap("dte_tok", dte_tok[:], [(EVV, EVV.v)])
        tap("cdecN", cdecN[:], [(EVV, EVV.v)])
        tap("S_core", carry[:], [(EVV, EVV.v)])


        if stop_after < 4:
            @block.sync
            def _(sync):
                sync.wait_ge(EVV.h, EVV.v)
                sync.wait_ge(EVS.h, EVS.v)
                sync.wait_ge(EVP.h, EVP.v)
                sync.wait_ge(MM.h, MM.v)
                sync.wait_ge(SPL.h, SPL.v)
                sync.wait_ge(STO.h, STO.v)
            _close_phase_stacks()
            return nc, tap_shapes

        ag_in = nc.dram_tensor("ag_in", [128, 4160], F32, kind="Internal").ap()
        ag_out = nc.dram_tensor("ag_out", [128 * NCORES, 4160], F32, kind="Internal",
                                addr_space="Shared").ap()
        @block.tensor
        def _(tensor):
            tensor.wait_ge(EVV.h, EVV.v)
            tensor.wait_ge(EVS.h, ev_cdec_prev[0])
            MM.inc(tensor.matmul(out=pF[:, 2048:2112],
                          lhsT=alsum[:, 0:1].broadcast_to([64, 128]),
                          rhs=id64f[:], start=True, stop=True))
        mm_dcore = MM.v

        @block.scalar
        def _(scalar):
            scalar.wait_ge(MM.h, mm_dcore)
            EVS.inc(scalar.activation(out=dcrN[:], in_=pF[:, 2048:2112], func=AF.Exp))
        ev_dcore = EVS.v

        tap("dcoreN", dcrN[:], [(EVS, ev_dcore)])

        @block.scalar
        def _(scalar):
            scalar.wait_ge(EVS.h, ev_dcore)
            SPL.inc(scalar.dma_start(out=ag_in[:, 0:4096], in_=carry[:]), 16)
            SPL.inc(scalar.dma_start(out=ag_in[:, 4096:4160], in_=dcrN[:]), 16)
        spl_ag = SPL.v

        @block.gpsimd
        def _(gpsimd, s0=SPLP[0].v, s1=SPLP[1].v):
            gpsimd.wait_ge(SPL.h, spl_ag)
            gpsimd.wait_ge(SPLP[0].h, s0)
            gpsimd.wait_ge(SPLP[1].h, s1)
            CC.inc(gpsimd.collective_compute(
                "AllGather", ALU.bypass,
                replica_groups=[list(range(NCORES))],
                ins=[ag_in.opt()], outs=[ag_out.opt()]))
        cc_done = CC.v

        esDT.close()
        esH = _mkstack()
        hornS = esH.enter_context(sb("hornS", [128, 4160], F32, side="right"))

        ev_horn = [0] * NCORES
        for j in range(NCORES):
            @block.gpsimd
            def _(gpsimd, j=j):
                if j == 0:
                    gpsimd.wait_ge(CC.h, cc_done)
                else:
                    gpsimd.wait_ge(EVV.h, ev_horn[j - 1])
                GDM.inc(gpsimd.dma_start(
                    out=hornS[:], in_=ag_out[j * 128:(j + 1) * 128, :]), 16)
            gdm_h = GDM.v

            @block.vector
            def _(vector, j=j, gdm_h=gdm_h):
                vector.wait_ge(GDM.h, gdm_h)
                if j == 0:
                    vector.wait_ge(SPL.h, spl_ag)
                    vector.memset(carry[:], 0.0)
                vector.tensor_scalar(out=dcrN[:], in0=hornS[:, 4096:4160],
                                     scalar1=emask[:, j:j + 1], scalar2=emask[:, 8 + j:9 + j],
                                     op0=ALU.mult, op1=ALU.add)
                vector.tensor_tensor(
                    out=carry[:].rearrange("q (h p2) -> q h p2", h=64),
                    in0=carry[:].rearrange("q (h p2) -> q h p2", h=64),
                    in1=dcrN[:].unsqueeze(2).broadcast_to([128, 64, 64]),
                    op=ALU.mult)
                ev_horn[j] = EVV.inc(vector.scalar_tensor_tensor(
                    out=carry[:], in0=hornS[:, 0:4096], scalar=emask[:, j:j + 1],
                    in1=carry[:], op0=ALU.mult, op1=ALU.add))
        esH.close()

        tap("carry_in", carry[:], [(EVV, EVV.v)])


        if stop_after < 5:
            @block.sync
            def _(sync):
                sync.wait_ge(EVV.h, EVV.v)
                sync.wait_ge(EVS.h, EVS.v)
                sync.wait_ge(EVP.h, EVP.v)
                sync.wait_ge(MM.h, MM.v)
                sync.wait_ge(SPL.h, SPL.v)
                sync.wait_ge(STO.h, STO.v)
            _close_phase_stacks()
            return nc, tap_shapes

        esYG = _mkstack()
        yg_f = esYG.enter_context(sb("yg_f", [128, 32 * T], BF16, side="left"))

        esC2 = _mkstack()
        EC = esC2.enter_context
        scrA = EC(sb("scrA", [128, 4096], BF16, side="right"))
        sgtok = EC(sb("sgtok", [128, 2048], BF16, side="right"))
        sgstage = EC(sb("sgstage", [128, 2048], BF16, side="right"))
        ygh0 = EC(sb("ygh0", [128, 2048], BF16, side="right"))
        ytmp = EC(sb("ytmp", [128, 512], F32, side="right"))
        datok = EC(sb("datok", [128, 64], BF16, side="right"))
        lntok = EC(sb("lntok", [128, 64], BF16, side="right"))
        eatok = EC(sb("eatok", [128, 64], F32, side="right"))
        ea64 = EC(sb("ea64", [64, 128], F32, side="right"))
        cbm = EC(sb("cbm", [128, 128], BF16, side="right"))
        ssqc = EC(sb("ssqc", [128, 8], F32, side="right"))
        ssum = EC(sb("ssum", [128, 1], F32, side="right"))

        ev_ptb_free = [0]
        mm_scrA_free = [0]
        ev_pf_low_free = [ev_carry_upd[0]]
        ev_small_evac = [0]

        for c in range(NT):
            cs = slice(c * 128, (c + 1) * 128)

            @block.scalar
            def _(scalar, c=c, cs=cs):
                if c == 0:
                    scalar.wait_ge(EVV.h, ev_horn[NCORES - 1])
                EVS.inc(scalar.activation(out=ea64[:], in_=acumT[:, cs], func=AF.Exp))
            ev_ea = EVS.v

            @block.tensor
            def _(tensor, c=c, cs=cs):
                tensor.wait_ge(EVS.h, ev_ea)
                if ev_small_evac[0]:
                    tensor.wait_ge(EVV.h, ev_small_evac[0])
                tensor.wait_ge(EVV.h, ev_ptb_free[0] if ev_ptb_free[0] else ev_xs_evac[0])
                tensor.transpose(out=pTB[:, 0:64], in_=dA_b[:, cs], identity=id64b[:])
                tensor.transpose(out=pTB[:, 64:128], in_=lndt_b[:, cs], identity=id64b[:])
                tensor.transpose(out=pF[:, 2048:2112], in_=ea64[:], identity=id64f[:])
                MM.inc(tensor.matmul(out=pF[:, 2176:2304], lhsT=b_n[:, cs], rhs=c_n[:, cs],
                              start=True, stop=True))
            mm_small = MM.v

            @block.vector
            def _(vector, c=c):
                vector.wait_ge(MM.h, mm_small)
                vector.tensor_copy(out=datok[:], in_=pTB[:, 0:64])
                vector.tensor_copy(out=lntok[:], in_=pTB[:, 64:128])
                vector.tensor_copy(out=eatok[:], in_=pF[:, 2048:2112])
                v = EVV.inc(vector.tensor_tensor(out=cbm[:], in0=pF[:, 2176:2304],
                                                 in1=trige[:], op=ALU.mult))
                ev_small_evac[0] = v
                ev_ptb_free[0] = v

            for hh in range(2):
                hoff = hh * 2048
                @block.sync
                def _(sync, c=c, hh=hh):
                    if ev_ptb_free[0]:
                        sync.wait_ge(EVV.h, ev_ptb_free[0])
                    src = sg_dram[16 * hh:16 * hh + 16, :, c * 128:(c + 1) * 128]
                    LD.inc(sync.dma_start(
                        out=sgstage[:].rearrange("q (f l) -> q f l", f=16),
                        in_=src.rearrange("m q t -> q m t")), 16)
                ld_sg = LD.v

                @block.tensor
                def _(tensor, c=c, hh=hh):
                    tensor.wait_ge(LD.h, ld_sg)
                    tensor.wait_ge(EVV.h, ev_ptb_free[0])
                    last = None
                    for ft in range(16):
                        last = tensor.transpose(out=pTB[:, ft * 128:(ft + 1) * 128],
                                         in_=sgstage[:, ft * 128:(ft + 1) * 128],
                                         identity=id128b[:])
                    MM.inc(last)
                mm_sgtr = MM.v

                @block.vector
                def _(vector, c=c, hh=hh, mm_sgtr=mm_sgtr):
                    vector.wait_ge(MM.h, mm_sgtr)
                    v = EVV.inc(vector.tensor_copy(out=sgtok[:], in_=pTB[:]))
                    ev_ptb_free[0] = v
                ev_sgtok = EVV.v

                @block.vector
                def _(vector, c=c, hh=hh):
                    vector.wait_ge(EVV.h, ev_small_evac[0])
                    if mm_scrA_free[0]:
                        vector.wait_ge(MM.h, mm_scrA_free[0])
                    EVV.inc(vector.tensor_tensor(
                        out=scrA[:].rearrange("q (h t) -> q h t", h=32),
                        in0=datok[:, hh * 32:hh * 32 + 32].unsqueeze(2).broadcast_to([128, 32, 128]),
                        in1=trile[:].unsqueeze(1).broadcast_to([128, 32, 128]),
                        op=ALU.mult))
                ev_rhs1 = EVV.v
                if c == 0 and hh == 0:
                    tap("dbg_rhs1", scrA[:], [(EVV, ev_rhs1)])
                    tap("dbg_datok", datok[:], [(EVV, ev_rhs1)])
                    tap("dbg_lntok", lntok[:], [(EVV, ev_rhs1)])
                    tap("dbg_eatok", eatok[:], [(EVV, ev_rhs1)])
                    tap("dbg_cbm", cbm[:], [(EVV, ev_rhs1)])
                    tap("dbg_sgtok", sgtok[:], [(EVV, ev_sgtok)])

                ev_exp_q = [0, 0]
                for q2 in range(2):
                    @block.tensor
                    def _(tensor, c=c, hh=hh, q2=q2):
                        tensor.wait_ge(EVV.h, ev_rhs1)
                        if q2 == 1:
                            tensor.wait_ge(EVS.h, ev_exp_q[0])
                        else:
                            tensor.wait_ge(EVV.h, ev_pf_low_free[0])
                        for nb in range(4):
                            col = q2 * 2048 + nb * 512
                            hsel = hh * 32 + q2 * 16 + nb * 4
                            tensor.matmul(out=pF[:, nb * 512:(nb + 1) * 512],
                                          lhsT=trigt[:], rhs=scrA[:, col:col + 512],
                                          start=True, stop=False)
                            last = tensor.matmul(out=pF[:, nb * 512:(nb + 1) * 512].rearrange(
                                              "q (h t) -> q h t", h=4),
                                          lhsT=id128b[:],
                                          rhs=lntok[:, hsel:hsel + 4].unsqueeze(2)
                                              .broadcast_to([128, 4, 128]),
                                          start=False, stop=True)
                        MM.inc(last)
                    mm_arg = MM.v

                    @block.scalar
                    def _(scalar, c=c, hh=hh, q2=q2, mm_arg=mm_arg):
                        scalar.wait_ge(MM.h, mm_arg)
                        ev_exp_q[q2] = EVS.inc(scalar.activation(
                            out=scrA[:, q2 * 2048:q2 * 2048 + 2048],
                            in_=pF[:, 0:2048], func=AF.Exp))

                @block.vector
                def _(vector, c=c, hh=hh):
                    vector.wait_ge(EVS.h, ev_exp_q[1])
                    EVV.inc(vector.tensor_tensor(
                        out=scrA[:].rearrange("q (h t) -> q h t", h=32),
                        in0=scrA[:].rearrange("q (h t) -> q h t", h=32),
                        in1=cbm[:].unsqueeze(1).broadcast_to([128, 32, 128]),
                        op=ALU.mult))
                ev_m = EVV.v
                if c == 0 and hh == 0:
                    tap("dbg_M", scrA[:], [(EVV, ev_m)])

                @block.vector
                def _(vector, c=c, hh=hh):
                    EVV.inc(vector.tensor_copy(out=shb4[:], in_=carry[:, hoff:hoff + 2048]))
                ev_cbf = EVV.v

                @block.tensor
                def _(tensor, c=c, hh=hh, cs=cs):
                    tensor.wait_ge(EVV.h, ev_cbf)
                    tensor.wait_ge(EVS.h, ev_exp_q[1])
                    last = None
                    for o in range(4):
                        last = tensor.matmul(out=pF[:, o * 512:(o + 1) * 512],
                                      lhsT=c_n[:, cs], rhs=shb4[:, o * 512:(o + 1) * 512],
                                      start=True, stop=True)
                    MM.inc(last)
                mm_y2 = MM.v

                @block.vector
                def _(vector, c=c, hh=hh, mm_y2=mm_y2):
                    vector.wait_ge(MM.h, mm_y2)
                    EVV.inc(vector.tensor_tensor(
                        out=shb4[:].rearrange("q (h p2) -> q h p2", h=32),
                        in0=xs_tok[:, c * I + hoff:c * I + hoff + 2048]
                            .rearrange("q (h p2) -> q h p2", h=32),
                        in1=d_n[:, hh * 32:hh * 32 + 32].unsqueeze(2).broadcast_to([128, 32, 64]),
                        op=ALU.mult))
                ev_xsd = EVV.v

                ev_oct = [0]
                for o in range(4):
                    @block.tensor
                    def _(tensor, c=c, hh=hh, o=o):
                        tensor.wait_ge(EVV.h, ev_m)
                        tensor.wait_ge(EVV.h, ev_xsd)
                        tensor.wait_ge(EVV.h, ev_sgtok)
                        if ev_oct[0]:
                            tensor.wait_ge(EVV.h, ev_oct[0])
                        for jh in range(8):
                            h8 = o * 8 + jh
                            tensor.matmul(out=pF[:, 2048 + jh * 64:2048 + (jh + 1) * 64],
                                          lhsT=scrA[:, h8 * 128:(h8 + 1) * 128],
                                          rhs=xs_tok[:, c * I + hoff + h8 * 64:
                                                     c * I + hoff + h8 * 64 + 64],
                                          start=True, stop=False)
                            last = tensor.matmul(out=pF[:, 2048 + jh * 64:2048 + (jh + 1) * 64],
                                          lhsT=id128b[:],
                                          rhs=shb4[:, h8 * 64:(h8 + 1) * 64],
                                          start=False, stop=True)
                        MM.inc(last)
                    mm_y1 = MM.v
                    mm_scrA_free[0] = mm_y1

                    @block.vector
                    def _(vector, c=c, hh=hh, o=o, mm_y1=mm_y1):
                        vector.wait_ge(MM.h, mm_y1)
                        vector.tensor_tensor(
                            out=ytmp[:].rearrange("q (h p2) -> q h p2", h=8),
                            in0=pF[:, o * 512:(o + 1) * 512].rearrange("q (h p2) -> q h p2", h=8),
                            in1=eatok[:, hh * 32 + o * 8:hh * 32 + o * 8 + 8].unsqueeze(2)
                                .broadcast_to([128, 8, 64]),
                            op=ALU.mult)
                        vector.tensor_add(out=ytmp[:], in0=ytmp[:], in1=pF[:, 2048:2560])
                        vector.tensor_tensor(out=scrA[:, o * 512:(o + 1) * 512],
                                             in0=ytmp[:], in1=sgtok[:, o * 512:(o + 1) * 512],
                                             op=ALU.mult)
                        ev_oct[0] = EVV.inc(vector.scalar_tensor_tensor(
                            out=ytmp[:], in0=scrA[:, o * 512:(o + 1) * 512], scalar=1.0,
                            in1=scrA[:, o * 512:(o + 1) * 512], op0=ALU.mult, op1=ALU.mult,
                            accum_out=ssqc[:, hh * 4 + o:hh * 4 + o + 1]))

                if hh == 0:
                    @block.vector
                    def _(vector, c=c):
                        EVV.inc(vector.tensor_copy(out=ygh0[:], in_=scrA[:, 0:2048]))
                    if c == 0:
                        tap("dbg_ygh0", ygh0[:], [(EVV, EVV.v)])
                ev_ygpark = EVV.v

                @block.vector
                def _(vector, c=c, hh=hh):
                    EVV.inc(vector.tensor_tensor(
                        out=shb4[:].rearrange("q (h p2) -> q h p2", h=32),
                        in0=xs_tok[:, c * I + hoff:c * I + hoff + 2048]
                            .rearrange("q (h p2) -> q h p2", h=32),
                        in1=dte_tok[:, c * 64 + hh * 32:c * 64 + hh * 32 + 32].unsqueeze(2)
                            .broadcast_to([128, 32, 64]),
                        op=ALU.mult))
                ev_dxs2 = EVV.v

                @block.tensor
                def _(tensor, c=c, hh=hh, cs=cs):
                    tensor.wait_ge(EVV.h, ev_dxs2)
                    last = None
                    for j in range(4):
                        last = tensor.matmul(out=pF[:, j * 512:(j + 1) * 512],
                                      lhsT=b_tok[:, cs],
                                      rhs=shb4[:, j * 512:(j + 1) * 512],
                                      start=True, stop=True)
                    MM.inc(last)
                mm_st2 = MM.v

                @block.vector
                def _(vector, c=c, hh=hh, mm_st2=mm_st2):
                    vector.wait_ge(MM.h, mm_st2)
                    ch = carry[:, hoff:hoff + 2048]
                    vector.tensor_tensor(
                        out=ch.rearrange("q (h p2) -> q h p2", h=32),
                        in0=ch.rearrange("q (h p2) -> q h p2", h=32),
                        in1=cdecN[:, c * 64 + hh * 32:c * 64 + hh * 32 + 32].unsqueeze(2)
                            .broadcast_to([128, 32, 64]),
                        op=ALU.mult)
                    v = EVV.inc(vector.tensor_add(out=ch, in0=ch, in1=pF[:, 0:2048]))
                    ev_pf_low_free[0] = v

            if c == 0:
                tap("dbg_ssqc", ssqc[:], [(EVV, EVV.v)])
            @block.gpsimd
            def _(gpsimd, c=c, evoct=ev_oct[0]):
                gpsimd.wait_ge(EVV.h, evoct)
                gpsimd.tensor_add(out=ssum[:], in0=ssqc[:, 0:1], in1=ssqc[:, 1:2])
                for qq in range(2, 8):
                    gpsimd.tensor_add(out=ssum[:], in0=ssum[:], in1=ssqc[:, qq:qq + 1])
                EVP.inc(gpsimd.tensor_scalar(out=ssum[:], in0=ssum[:], scalar1=1.0 / I,
                                             scalar2=EPS, op0=ALU.mult, op1=ALU.add))
            ev_var = EVP.v
            if c == 0:
                tap("dbg_ssum", ssum[:], [(EVP, ev_var)])

            @block.scalar
            def _(scalar, c=c, ev_var=ev_var):
                scalar.wait_ge(EVP.h, ev_var)
                scalar.activation(out=rinv8[:, c:c + 1], in_=ssum[:], func=AF.Ln)
                EVS.inc(scalar.activation(out=rinv8[:, c:c + 1], in_=rinv8[:, c:c + 1],
                                          func=AF.Exp, scale=-0.5))
            ev_rinv = EVS.v

            for hh in range(2):
                srcbuf = ygh0 if hh == 0 else scrA

                @block.vector
                def _(vector, c=c, hh=hh, srcbuf=srcbuf, ev_rinv=ev_rinv):
                    vector.wait_ge(EVS.h, ev_rinv)
                    EVV.inc(vector.tensor_scalar_mul(
                        out=srcbuf[:, 0:2048], in0=srcbuf[:, 0:2048],
                        scalar1=rinv8[:, c:c + 1]))
                ev_rapp = EVV.v

                @block.tensor
                def _(tensor, c=c, hh=hh, srcbuf=srcbuf, ev_rapp=ev_rapp):
                    tensor.wait_ge(EVV.h, ev_rapp)
                    tensor.wait_ge(EVV.h, ev_ptb_free[0])
                    last = None
                    for ft in range(16):
                        last = tensor.transpose(out=pTB[:, ft * 128:(ft + 1) * 128],
                                         in_=srcbuf[:, ft * 128:(ft + 1) * 128],
                                         identity=id128b[:])
                    MM.inc(last)
                mm_ygtr = MM.v
                if hh == 1:
                    mm_scrA_free[0] = mm_ygtr

                @block.vector
                def _(vector, c=c, hh=hh, mm_ygtr=mm_ygtr):
                    vector.wait_ge(MM.h, mm_ygtr)
                    last = None
                    for ft in range(16):
                        fb = hh * 16 + ft
                        last = vector.tensor_scalar_mul(
                            out=yg_f[:, fb * T + c * 128:fb * T + (c + 1) * 128],
                            in0=pTB[:, ft * 128:(ft + 1) * 128],
                            scalar1=normw[:, fb:fb + 1])
                    ev_ptb_free[0] = EVV.inc(last)

        ev_c2 = EVV.v
        tap("yg_f", yg_f[:, 0:T], [(EVV, ev_c2)])
        tap("rinv8", rinv8[:], [(EVV, ev_c2)])

        esC2.close()
        esB.close()


        if stop_after < 6:
            @block.sync
            def _(sync):
                sync.wait_ge(EVV.h, EVV.v)
                sync.wait_ge(EVS.h, EVS.v)
                sync.wait_ge(EVP.h, EVP.v)
                sync.wait_ge(MM.h, MM.v)
                sync.wait_ge(SPL.h, SPL.v)
                sync.wait_ge(STO.h, STO.v)
            _close_phase_stacks()
            return nc, tap_shapes

        esD = _mkstack()
        ED = esD.enter_context
        z1 = ED(sb("z1", [128, NT * HID], F32, side="right"))
        wraw2 = [ED(sb(f"wraw2{i}", [128, 4096], F32, side="right")) for i in range(2)]
        wbf2 = [ED(sb(f"wbf2{i}", [128, 4096], BF16, side="right")) for i in range(2)]
        ztmp = ED(sb("ztmp", [128, 512], F32, side="right"))
        lnscr = ED(sb("lnscr", [128, 2048], F32, side="right"))
        x2bstg = ED(sb("x2bstg", [128, 2048], BF16, side="right"))

        @block.sync
        def _(sync, e1=EVV.v, e2=EVS.v, e3=EVP.v, e4=MM.v):
            sync.wait_ge(EVV.h, e1)
            sync.wait_ge(EVS.h, e2)
            sync.wait_ge(EVP.h, e3)
            sync.wait_ge(MM.h, e4)
            for i in range(NT):
                LD.inc(sync.dma_start(out=z1[:, i * HID:(i + 1) * HID],
                                      in_=x_hbm[i * 128:(i + 1) * 128, :]), 16)
        ld_z1 = LD.v

        if "dbg_wout_hbm" in taps:
            th_w = nc.dram_tensor("tap_dbg_wout_hbm", [128, 4096], F32, kind="ExternalOutput")
            tap_shapes["dbg_wout_hbm"] = (128, 4096)

            @block.sync
            def _(sync):
                STO.inc(sync.dma_start(out=th_w.ap(), in_=wout_hbm[1]), 16)

        wldD = [0] * 16
        cvtD = [0] * 16
        mmD = {}
        mm_stripD = [0] * 16
        evdD = {}
        ev_ztr_evac = [0]
        mm_ztrD = [0]
        for m in range(16):
            @block.sync
            def _(sync, m=m):
                if m >= 2:
                    sync.wait_ge(CVT.h, cvtD[m - 2])
                wldD[m] = WLDP[m % 2].inc(sync.dma_start(out=wraw2[m % 2][:], in_=wout_hbm[m]), 16)

            @block.gpsimd
            def _(gpsimd, m=m):
                gpsimd.wait_ge(WLDP[m % 2].h, wldD[m])
                if m >= 2:
                    gpsimd.wait_ge(MM.h, mm_stripD[m - 2])
                cvtD[m] = CVT.inc(gpsimd.tensor_copy(out=wbf2[m % 2][:], in_=wraw2[m % 2][:]))


            @block.tensor
            def _(tensor, m=m):
                tensor.wait_ge(CVT.h, cvtD[m])
                if m == 0:
                    tensor.wait_ge(EVV.h, ev_c2)
                for tb in range(2):
                    bk = (m % 2) * 2 + tb
                    if (m - 2, tb) in evdD:
                        s, v = evdD[(m - 2, tb)]
                        tensor.wait_ge(s.h, v)
                    last = None
                    for k in range(32):
                        last = tensor.matmul(out=pF[:, bk * 512:(bk + 1) * 512],
                                      lhsT=wbf2[m % 2][:, k * 128:(k + 1) * 128],
                                      rhs=yg_f[:, k * T + tb * 512:k * T + tb * 512 + 512],
                                      start=(k == 0), stop=(k == 31))
                    mmD[(m, tb)] = MM.inc(last)
                mm_stripD[m] = MM.v

            for tb in range(2):
                bk = (m % 2) * 2 + tb

                @block.vector
                def _(vector, m=m, tb=tb, bk=bk):
                    vector.wait_ge(MM.h, mmD[(m, tb)])
                    if mm_ztrD[0]:
                        vector.wait_ge(MM.h, mm_ztrD[0])
                    v = EVV.inc(vector.tensor_copy(out=ztmp[:], in_=pF[:, bk * 512:(bk + 1) * 512]))
                    evdD[(m, tb)] = (EVV, v)
                ev_zt = EVV.v
                if m == 1 and tb == 0:
                    tap("dbg_ztmp", ztmp[:], [(EVV, ev_zt)])

                @block.tensor
                def _(tensor, m=m, tb=tb, ev_zt=ev_zt):
                    tensor.wait_ge(EVV.h, ev_zt)
                    if ev_ztr_evac[0]:
                        tensor.wait_ge(EVV.h, ev_ztr_evac[0])
                    last = None
                    for q in range(4):
                        last = tensor.transpose(out=pF[:, 2048 + q * 128:2048 + (q + 1) * 128],
                                         in_=ztmp[:, q * 128:(q + 1) * 128],
                                         identity=id128f[:])
                    MM.inc(last)
                mm_ztrD[0] = MM.v
                mm_ztr_here = MM.v

                @block.vector
                def _(vector, m=m, tb=tb, mm_ztr_here=mm_ztr_here):
                    vector.wait_ge(MM.h, mm_ztr_here)
                    vector.wait_ge(LD.h, ld_z1)
                    last = None
                    for q in range(4):
                        tile = tb * 4 + q
                        last = vector.tensor_add(
                            out=z1[:, tile * HID + m * 128:tile * HID + (m + 1) * 128],
                            in0=pF[:, 2048 + q * 128:2048 + (q + 1) * 128],
                            in1=z1[:, tile * HID + m * 128:tile * HID + (m + 1) * 128])
                    ev_ztr_evac[0] = EVV.inc(last)

        ev_wout_done = EVV.v
        tap("dbg_z1", z1[:, 0:HID], [(EVV, ev_wout_done)])
        tap("dbg_wraw2b", wraw2[0][:], [(EVV, ev_wout_done)])
        tap("dbg_wbf2b", wbf2[0][:], [(EVV, ev_wout_done)])
        esYG.close()

        @block.vector
        def _(vector):
            vector.wait_ge(EVV.h, ev_wout_done)
            for tt in range(NT):
                zt = z1[:, tt * HID:(tt + 1) * HID]
                vector.scalar_tensor_tensor(out=lnscr[:], in0=zt, scalar=1.0, in1=zt,
                                            op0=ALU.mult, op1=ALU.mult,
                                            accum_out=ssq16[:, tt:tt + 1])
                vector.tensor_scalar(out=lnscr[:], in0=zt, scalar1=0.0, scalar2=0.0,
                                     op0=ALU.add, op1=ALU.add,
                                     accum_out=sums16[:, tt:tt + 1])
            EVV.inc(vector.engine_nop())
        ev_ln1acc = EVV.v

        @block.gpsimd
        def _(gpsimd):
            gpsimd.wait_ge(EVV.h, ev_ln1acc)
            gpsimd.tensor_scalar_mul(out=mucol[:, 0:8], in0=sums16[:, 0:8], scalar1=1.0 / HID)
            gpsimd.tensor_scalar_mul(out=rstd[:, 0:8], in0=ssq16[:, 0:8], scalar1=1.0 / HID)
            gpsimd.tensor_mul(out=sums16[:, 0:8], in0=mucol[:, 0:8], in1=mucol[:, 0:8])
            gpsimd.tensor_sub(out=rstd[:, 0:8], in0=rstd[:, 0:8], in1=sums16[:, 0:8])
            EVP.inc(gpsimd.tensor_scalar_add(out=rstd[:, 0:8], in0=rstd[:, 0:8], scalar1=EPS))
        ev_ln1v = EVP.v

        @block.scalar
        def _(scalar):
            scalar.wait_ge(EVP.h, ev_ln1v)
            scalar.activation(out=rstd[:, 0:8], in_=rstd[:, 0:8], func=AF.Ln)
            EVS.inc(scalar.activation(out=rstd[:, 0:8], in_=rstd[:, 0:8], func=AF.Exp,
                                      scale=-0.5))
        ev_ln1s = EVS.v

        @block.vector
        def _(vector):
            vector.wait_ge(EVS.h, ev_ln1s)
            last = None
            for tt in range(NT):
                zt = z1[:, tt * HID:(tt + 1) * HID]
                last = vector.tensor_scalar(out=zt, in0=zt, scalar1=mucol[:, tt:tt + 1],
                                            scalar2=rstd[:, tt:tt + 1],
                                            op0=ALU.subtract, op1=ALU.mult)
            EVV.inc(last)
        ev_x2 = EVV.v

        tap("x2", z1[:, 0:HID], [(EVV, ev_x2)])

        esE_L = _mkstack()
        x2f = esE_L.enter_context(sb("x2f", [128, 16 * T], BF16, side="left"))

        @block.sync
        def _(sync):
            sync.wait_ge(EVV.h, ev_x2)
            for tt in range(NT):
                SPL.inc(sync.dma_start(out=x2_dram[tt * 128:(tt + 1) * 128, :],
                                       in_=z1[:, tt * HID:(tt + 1) * HID]), 16)
        spl_x2 = SPL.v

        mm_x2tr = [0]
        ev_x2evac = [0]
        for tt in range(NT):
            @block.vector
            def _(vector, tt=tt):
                if tt == 0:
                    vector.wait_ge(EVV.h, ev_x2)
                if mm_x2tr[0]:
                    vector.wait_ge(MM.h, mm_x2tr[0])
                EVV.inc(vector.tensor_copy(out=x2bstg[:], in_=z1[:, tt * HID:(tt + 1) * HID]))
            ev_x2c = EVV.v

            @block.tensor
            def _(tensor, tt=tt, ev_x2c=ev_x2c):
                tensor.wait_ge(EVV.h, ev_x2c)
                tensor.wait_ge(EVV.h, ev_x2evac[0] if ev_x2evac[0] else ev_ptb_free[0])
                last = None
                for k in range(16):
                    last = tensor.transpose(out=pTB[:, k * 128:(k + 1) * 128],
                                     in_=x2bstg[:, k * 128:(k + 1) * 128],
                                     identity=id128b[:])
                mm_x2tr[0] = MM.inc(last)
            mmx = MM.v

            @block.vector
            def _(vector, tt=tt, mmx=mmx):
                vector.wait_ge(MM.h, mmx)
                dst = x2f[:].rearrange("q (k t) -> q k t", k=16)[:, :, tt * 128:(tt + 1) * 128]
                ev_x2evac[0] = EVV.inc(vector.tensor_copy(
                    out=dst, in_=pTB[:].rearrange("q (k t) -> q k t", k=16)))

        esD.close()


        if stop_after < 7:
            @block.sync
            def _(sync):
                sync.wait_ge(EVV.h, EVV.v)
                sync.wait_ge(EVS.h, EVS.v)
                sync.wait_ge(EVP.h, EVP.v)
                sync.wait_ge(MM.h, MM.v)
                sync.wait_ge(SPL.h, SPL.v)
                sync.wait_ge(STO.h, STO.v)
            _close_phase_stacks()
            return nc, tap_shapes

        h_bf = esE_L.enter_context(sb("h_bf", [128, 64 * 512], BF16, side="left"))
        esE = _mkstack()
        EE = esE.enter_context
        wrawE = [EE(sb(f"wrawE{i}", [128, 4096], F32, side="right")) for i in range(2)]
        wbfE = [EE(sb(f"wbfE{i}", [128, 4096], BF16, side="right")) for i in range(2)]
        sgu = EE(sb("sgu", [128, 512], F32, side="right"))
        z2h = EE(sb("z2h", [128, 4 * HID], F32, side="right"))
        dnraw = [EE(sb(f"dnraw{i}", [128, 512], F32, side="right")) for i in range(2)]
        dnbf = [EE(sb(f"dnbf{i}", [128, 512], BF16, side="right")) for i in range(2)]
        ztmpE = EE(sb("ztmpE", [128, 512], F32, side="right"))
        lnscr2 = EE(sb("lnscr2", [128, 2048], F32, side="right"))

        ev_h_prev = {}
        mm_dn_last = [0]
        ev_dn_evac = [0]
        ev_dn_tr = [0]
        ev_silu_last = [0]

        @block.sync
        def _(sync, e1=EVV.v, e2=EVS.v, e3=EVP.v, e4=MM.v, e5=SPL.v):
            sync.wait_ge(EVV.h, e1)
            sync.wait_ge(EVS.h, e2)
            sync.wait_ge(EVP.h, e3)
            sync.wait_ge(MM.h, e4)
            sync.wait_ge(SPL.h, e5)

        for hf in range(2):
            @block.sync
            def _(sync, hf=hf):
                if hf == 0:
                    sync.wait_ge(SPL.h, spl_x2)
                else:
                    sync.wait_ge(STO.h, STO.v)
                for q in range(4):
                    tt = hf * 4 + q
                    LD.inc(sync.dma_start(out=z2h[:, q * HID:(q + 1) * HID],
                                          in_=x2_dram[tt * 128:(tt + 1) * 128, :]), 16)
            ld_z2 = LD.v

            cvtE = [0] * 64
            mmE1 = [0] * 64
            for j in range(64):
                @block.sync
                def _(sync, hf=hf, j=j, snap=CVT.v):
                    sync.wait_ge(CVT.h, cvtE[j - 2] if j >= 2 else snap)
                    WLDP[j % 2].inc(sync.dma_start(out=wrawE[j % 2][:, 0:2048], in_=gu_hbm[j]), 16)
                    WLDP[j % 2].inc(sync.dma_start(out=wrawE[j % 2][:, 2048:4096], in_=gu_hbm[64 + j]), 16)
                wldE = WLDP[j % 2].v

                @block.gpsimd
                def _(gpsimd, hf=hf, j=j, wldE=wldE):
                    gpsimd.wait_ge(WLDP[j % 2].h, wldE)
                    if j >= 2:
                        gpsimd.wait_ge(MM.h, mmE1[j - 2])
                    cvtE[j] = CVT.inc(gpsimd.tensor_copy(out=wbfE[j % 2][:], in_=wrawE[j % 2][:]))

                @block.tensor
                def _(tensor, hf=hf, j=j):
                    tensor.wait_ge(CVT.h, cvtE[j])
                    if j >= 2:
                        tensor.wait_ge(EVV.h, ev_h_prev[(hf, j - 2)])
                    elif hf == 1 or ev_dn_evac[0]:
                        tensor.wait_ge(EVV.h, ev_dn_evac[0])
                    last = None
                    for part in range(2):
                        bk = (j % 2) * 2 + part
                        for k in range(16):
                            last = tensor.matmul(
                                out=pF[:, bk * 512:(bk + 1) * 512],
                                lhsT=wbfE[j % 2][:, part * 2048 + k * 128:part * 2048 + (k + 1) * 128],
                                rhs=x2f[:, k * T + hf * 512:k * T + hf * 512 + 512],
                                start=(k == 0), stop=(k == 15))
                    mmE1[j] = MM.inc(last)

                @block.scalar
                def _(scalar, hf=hf, j=j):
                    scalar.wait_ge(MM.h, mmE1[j])
                    EVS.inc(scalar.activation(out=sgu[:], in_=pF[:, ((j % 2) * 2) * 512:
                                                                ((j % 2) * 2 + 1) * 512],
                                              func=AF.Silu))
                ev_siluE = EVS.v

                ev_silu_last[0] = ev_siluE

                @block.vector
                def _(vector, hf=hf, j=j, ev_siluE=ev_siluE):
                    vector.wait_ge(EVS.h, ev_siluE)
                    if hf == 1 and j == 0:
                        vector.wait_ge(MM.h, mm_dn_last[0])
                    bku = (j % 2) * 2 + 1
                    ev_h_prev[(hf, j)] = EVV.inc(vector.tensor_tensor(
                        out=h_bf[:, j * 512:(j + 1) * 512], in0=sgu[:],
                        in1=pF[:, bku * 512:(bku + 1) * 512], op=ALU.mult))
            for dg in range(4):
                cvtDn = [0] * 64
                mmDn = [0] * 64
                for j in range(64):
                    @block.sync
                    def _(sync, hf=hf, dg=dg, j=j, snap=CVT.v):
                        sync.wait_ge(CVT.h, cvtDn[j - 2] if j >= 2 else snap)
                        DNP[j % 2].inc(sync.dma_start(out=dnraw[j % 2][:], in_=dn_hbm[dg, j]), 16)
                    wldDn = DNP[j % 2].v

                    @block.gpsimd
                    def _(gpsimd, hf=hf, dg=dg, j=j, wldDn=wldDn):
                        gpsimd.wait_ge(DNP[j % 2].h, wldDn)
                        if j >= 2:
                            gpsimd.wait_ge(MM.h, mmDn[j - 2])
                        cvtDn[j] = CVT.inc(gpsimd.tensor_copy(out=dnbf[j % 2][:],
                                                              in_=dnraw[j % 2][:]))

                    @block.tensor
                    def _(tensor, hf=hf, dg=dg, j=j):
                        tensor.wait_ge(CVT.h, cvtDn[j])
                        if j == 0:
                            tensor.wait_ge(EVV.h, ev_h_prev[(hf, 63)])
                            tensor.wait_ge(EVS.h, ev_silu_last[0])
                            if ev_dn_evac[0]:
                                tensor.wait_ge(EVV.h, ev_dn_evac[0])
                        last = None
                        for mq in range(4):
                            last = tensor.matmul(out=pF[:, mq * 512:(mq + 1) * 512],
                                          lhsT=dnbf[j % 2][:, mq * 128:(mq + 1) * 128],
                                          rhs=h_bf[:, j * 512:(j + 1) * 512],
                                          start=(j == 0), stop=(j == 63))
                        mmDn[j] = MM.inc(last)

                mm_dg = MM.v
                mm_dn_last[0] = mm_dg
                for mq in range(4):
                    m = dg * 4 + mq

                    @block.vector
                    def _(vector, hf=hf, m=m, mq=mq, mm_dg=mm_dg):
                        vector.wait_ge(MM.h, mm_dg)
                        if ev_dn_tr[0]:
                            vector.wait_ge(MM.h, ev_dn_tr[0])
                        EVV.inc(vector.tensor_copy(out=ztmpE[:], in_=pF[:, mq * 512:(mq + 1) * 512]))
                    ev_ze = EVV.v

                    @block.tensor
                    def _(tensor, hf=hf, m=m, ev_ze=ev_ze):
                        tensor.wait_ge(EVV.h, ev_ze)
                        if ev_dn_evac[0]:
                            tensor.wait_ge(EVV.h, ev_dn_evac[0])
                        last = None
                        for q in range(4):
                            last = tensor.transpose(out=pF[:, 2048 + q * 128:2048 + (q + 1) * 128],
                                             in_=ztmpE[:, q * 128:(q + 1) * 128],
                                             identity=id128f[:])
                        MM.inc(last)
                        ev_dn_tr[0] = MM.v
                    mm_tr = MM.v

                    @block.vector
                    def _(vector, hf=hf, m=m, mm_tr=mm_tr, ld_z2=ld_z2):
                        vector.wait_ge(MM.h, mm_tr)
                        vector.wait_ge(LD.h, ld_z2)
                        last = None
                        for q in range(4):
                            last = vector.tensor_add(
                                out=z2h[:, q * HID + m * 128:q * HID + (m + 1) * 128],
                                in0=pF[:, 2048 + q * 128:2048 + (q + 1) * 128],
                                in1=z2h[:, q * HID + m * 128:q * HID + (m + 1) * 128])
                        ev_dn_evac[0] = EVV.inc(last)

            @block.vector
            def _(vector, hf=hf):
                vector.wait_ge(EVV.h, EVV.v)
                for q in range(4):
                    zt = z2h[:, q * HID:(q + 1) * HID]
                    vector.scalar_tensor_tensor(out=lnscr2[:], in0=zt, scalar=1.0, in1=zt,
                                                op0=ALU.mult, op1=ALU.mult,
                                                accum_out=ssq16[:, 8 + q:9 + q])
                    vector.tensor_scalar(out=lnscr2[:], in0=zt, scalar1=0.0, scalar2=0.0,
                                         op0=ALU.add, op1=ALU.add,
                                         accum_out=sums16[:, 8 + q:9 + q])
                EVV.inc(vector.engine_nop())
            ev_ln2acc = EVV.v

            @block.gpsimd
            def _(gpsimd, hf=hf, ev_ln2acc=ev_ln2acc):
                gpsimd.wait_ge(EVV.h, ev_ln2acc)
                gpsimd.tensor_scalar_mul(out=mucol[:, 8:12], in0=sums16[:, 8:12], scalar1=1.0 / HID)
                gpsimd.tensor_scalar_mul(out=rstd[:, 8:12], in0=ssq16[:, 8:12], scalar1=1.0 / HID)
                gpsimd.tensor_mul(out=sums16[:, 8:12], in0=mucol[:, 8:12], in1=mucol[:, 8:12])
                gpsimd.tensor_sub(out=rstd[:, 8:12], in0=rstd[:, 8:12], in1=sums16[:, 8:12])
                EVP.inc(gpsimd.tensor_scalar_add(out=rstd[:, 8:12], in0=rstd[:, 8:12],
                                                 scalar1=EPS))
            ev_ln2v = EVP.v

            @block.scalar
            def _(scalar, hf=hf, ev_ln2v=ev_ln2v):
                scalar.wait_ge(EVP.h, ev_ln2v)
                scalar.activation(out=rstd[:, 8:12], in_=rstd[:, 8:12], func=AF.Ln)
                EVS.inc(scalar.activation(out=rstd[:, 8:12], in_=rstd[:, 8:12], func=AF.Exp,
                                          scale=-0.5))
            ev_ln2s = EVS.v

            @block.vector
            def _(vector, hf=hf, ev_ln2s=ev_ln2s):
                vector.wait_ge(EVS.h, ev_ln2s)
                last = None
                for q in range(4):
                    zt = z2h[:, q * HID:(q + 1) * HID]
                    last = vector.tensor_scalar(out=zt, in0=zt, scalar1=mucol[:, 8 + q:9 + q],
                                                scalar2=rstd[:, 8 + q:9 + q],
                                                op0=ALU.subtract, op1=ALU.mult)
                EVV.inc(last)
            ev_out = EVV.v

            @block.sync
            def _(sync, hf=hf, ev_out=ev_out):
                sync.wait_ge(EVV.h, ev_out)
                for q in range(4):
                    tt = hf * 4 + q
                    STO.inc(sync.dma_start(out=out_hbm[tt * 128:(tt + 1) * 128, :],
                                           in_=z2h[:, q * HID:(q + 1) * HID]), 16)

        esE.close()
        esE_L.close()

        @block.sync
        def _(sync):
            sync.wait_ge(STO.h, STO.v)

    return nc, tap_shapes


def _strips(wT, nk, nm):
    a = wT.reshape(nk, 128, nm, 128).transpose(2, 1, 0, 3).reshape(nm, 128, nk * 128)
    return np.ascontiguousarray(a)


def prepare_in_maps(L0_attn_x, L0_attn_W_in, L0_attn_conv_w, L0_attn_conv_b,
                    L0_attn_dt_bias, L0_attn_A_log, L0_attn_D, L0_attn_norm_w,
                    L0_attn_W_out, L0_ffn_w_gate_up, L0_ffn_w_down):
    x2d = np.ascontiguousarray(np.asarray(L0_attn_x).reshape(8192, HID))

    winT = np.zeros((HID, MC_ALL * 128), dtype=np.float32)
    winT[:, :PROJ] = np.asarray(L0_attn_W_in).T
    w_in_s = _strips(winT, 16, MC_ALL)

    w_out_s = _strips(np.ascontiguousarray(np.asarray(L0_attn_W_out).T), 32, 16)
    w_gu_s = _strips(np.ascontiguousarray(np.asarray(L0_ffn_w_gate_up).T), 16, 128)

    dnT = np.ascontiguousarray(np.asarray(L0_ffn_w_down).T)
    w_dn_s = np.ascontiguousarray(
        dnT.reshape(64, 128, 4, 4, 128).transpose(2, 0, 1, 3, 4).reshape(4, 64, 128, 512))

    conv_w = np.ascontiguousarray(
        np.asarray(L0_attn_conv_w).reshape(MC_HBC, 128, 4).transpose(1, 0, 2).reshape(128, MC_HBC * 4))
    conv_b = np.ascontiguousarray(np.asarray(L0_attn_conv_b).reshape(MC_HBC, 128).T)
    dt_bias = np.ascontiguousarray(np.asarray(L0_attn_dt_bias).reshape(64, 1))
    a_log = np.ascontiguousarray(np.asarray(L0_attn_A_log).reshape(64, 1))
    d_nn = np.ascontiguousarray(np.broadcast_to(np.asarray(L0_attn_D), (128, 64)))
    normw_g = np.ascontiguousarray(np.asarray(L0_attn_norm_w).reshape(32, 128).T)

    shared = dict(w_in_s=w_in_s, w_out_s=w_out_s, w_gu_s=w_gu_s, w_dn_s=w_dn_s,
                  conv_w=conv_w, conv_b=conv_b, dt_bias=dt_bias, a_log=a_log,
                  d_n=d_nn, normw_g=normw_g)

    in_maps = []
    for c in range(NCORES):
        e = (np.arange(8) < c).astype(np.float32)
        em = np.concatenate([e, 1.0 - e])
        emask = np.ascontiguousarray(np.broadcast_to(em, (128, 16)).astype(np.float32))
        xprev = np.zeros((3, HID), np.float32) if c == 0 else \
            np.ascontiguousarray(x2d[c * T - 3:c * T])
        m = dict(shared)
        m.update(x_tok=np.ascontiguousarray(x2d[c * T:(c + 1) * T]),
                 xprev=xprev, emask=emask)
        in_maps.append(m)
    return in_maps


def kernel(**inputs):
    in_maps = prepare_in_maps(**inputs)
    nc, _ = build_program()
    res = run_bass_kernel_spmd(nc, in_maps, core_ids=list(range(NCORES)))
    out = np.concatenate([r["out"] for r in res.results], axis=0)
    return out.reshape(1, 8192, HID)
